# version 29
# baseline (speedup 1.0000x reference)
import numpy as np
import ml_dtypes
from contextlib import ExitStack

import concourse.bass as bass
import concourse.mybir as mybir
from concourse.bass_utils import run_bass_kernel_spmd

F32 = mybir.dt.float32
BF16 = mybir.dt.bfloat16
AF = mybir.ActivationFunctionType
ALU = mybir.AluOpType
BF = ml_dtypes.bfloat16
AX = mybir.AxisListType.X

BATCH, NCORES, SEQ, D, H, HD = 16, 8, 2048, 1024, 8, 128
C, L, MACRO, FFN = 64, 128, 256, 4096
NSUB, NCH = MACRO // 128, MACRO // C
SEQ_PER_CORE = BATCH // NCORES
MACROS_PER_SEQ = SEQ // MACRO
EPS_LN, EPS_K, EPS_RMS = 1e-5, 1e-6, 1e-6

ENGS = (("T", "V", "S", "P", "Y", "DC", "DW0", "DW1", "DX", "DSP", "DO", "DT2", "DP0", "DP1", "DB") + tuple(f"DS{i}" for i in range(8)) + tuple(f"DU{i}" for i in range(8)) + ("DX1", "DXB0", "DXB1") + tuple(f"DK{i}" for i in range(8)))


class _Sem:
    def __init__(self, nc, name):
        self.name, self.cm, self.h, self.n = name, nc.semaphore(name), None, 0


class Buf:
    __slots__ = ("name", "last_w", "readers")

    def __init__(self, name):
        self.name, self.last_w, self.readers = name, None, []


class Bld:

    def __init__(self, nc):
        self.nc = nc
        self.sems = {e: _Sem(nc, f"sem_{e}") for e in ENGS}
        self.q = {e: [] for e in ENGS}
        self.waited = {}
        self.pending = {e: [] for e in ENGS}

    def enter(self, ctx):
        for s in self.sems.values():
            s.h = ctx.enter_context(s.cm)

    def _emit_wait(self, eng, tok):
        if tok is None:
            return
        peng, sem, val = tok
        if val is None:
            assert peng == eng, f"cross-engine read of pending token ({sem.name})"
            return
        if peng == eng and eng == "T":
            return
        key = (eng, sem.name)
        if self.waited.get(key, 0) >= val:
            return
        self.waited[key] = val
        self.q[eng].append(lambda e, s=sem, v=val: e.wait_ge(s.h, v))

    def op(self, eng, fname, *args, reads=(), writes=(), inc=True, dma=False,
           dsem=None, **kw):
        for bb in reads:
            self._emit_wait(eng, bb.last_w)
        for bb in writes:
            self._emit_wait(eng, bb.last_w)
            for t in bb.readers:
                self._emit_wait(eng, t)
        semname = dsem if dsem is not None else eng
        sem = self.sems[semname]
        amt = 16 if dma else 1
        if inc:
            sem.n += amt
            tok = (semname, sem, sem.n)
        else:
            tok = None

        def emit(e, fname=fname, args=args, kw=kw, tok=tok, amt=amt):
            inst = getattr(e, fname)(*args, **kw)
            if tok is not None:
                inst.then_inc(tok[1].h, amt)
            return inst

        self.q[eng].append(emit)
        if tok is not None:
            for bb in self.pending[eng]:
                if bb.last_w is not None and bb.last_w[2] is None:
                    bb.last_w = tok
                else:
                    bb.readers.append(tok)
            self.pending[eng] = []
            for bb in reads:
                bb.readers.append(tok)
            for bb in writes:
                bb.last_w, bb.readers = tok, []
        else:
            for bb in reads:
                self.pending[eng].append(bb)
            for bb in writes:
                bb.last_w, bb.readers = (eng, sem, None), []
                self.pending[eng].append(bb)
        return tok

    def settle(self, semname, bufs):
        s = self.sems[semname]
        for bb in bufs:
            bb.last_w = (semname, s, s.n)

    def wait_self(self, eng):
        s = self.sems[eng]
        if s.n:
            key = (eng, s.name)
            if self.waited.get(key, 0) < s.n:
                self.waited[key] = s.n
                self.q[eng].append(lambda e, ss=s, v=s.n: e.wait_ge(ss.h, v))

    def wait_everything(self, eng):
        for e2 in ENGS:
            s = self.sems[e2]
            if s.n:
                if e2 == eng:
                    self.wait_self(eng)
                else:
                    self._emit_wait(eng, (e2, s, s.n))

    def flush(self, blk):
        sect = {"T": blk.tensor, "V": blk.vector, "S": blk.scalar,
                "P": blk.gpsimd, "Y": blk.sync}
        for e in ("T", "V", "S", "P", "Y"):
            ops = self.q[e]
            if ops:
                def body(engine, ops=tuple(ops)):
                    for cfn in ops:
                        cfn(engine)
                sect[e](body)
                self.q[e] = []


def _masks():
    t = np.arange(1, C + 1)
    lt = (t[:, None] > t[None, :])
    le = (t[:, None] >= t[None, :])
    mask_T = np.zeros((L, L), np.float32)
    mask_T[:C, :C] = lt; mask_T[:C, C:] = lt
    mask_T[C:, :C] = le; mask_T[C:, C:] = lt
    mask_W = np.concatenate([lt.T, lt.T], axis=1)
    mask_HQ = np.concatenate([le.T, le.T], axis=0)
    mask_KQ = le.T
    return mask_T, mask_T.T, mask_W, mask_HQ, mask_KQ


def build_program(n_macros=None, taps=()):
    nc = bass.Bass("TRN2", target_bir_lowering=False)
    total_macros = SEQ_PER_CORE * MACROS_PER_SEQ
    if n_macros is None:
        n_macros = total_macros

    x_h = nc.dram_tensor("x", [SEQ_PER_CORE, SEQ, D], F32, kind="ExternalInput")
    wh = {}
    for nm, shp in [("Wq", [D, D]), ("Wk", [D, D]), ("Wv", [D, D]),
                    ("Wf1", [D, HD]), ("Wf2", [HD, D]), ("Wbeta", [D, H]),
                    ("Wog1", [D, HD]), ("Wog2", [HD, D]), ("norm_w", [D]),
                    ("Wo", [D, D]), ("w1", [D, FFN]), ("b1", [FFN]),
                    ("w2", [FFN, D]), ("b2", [D])]:
        wh[nm] = nc.dram_tensor(nm, shp, F32, kind="ExternalInput")
    out_h = nc.dram_tensor("out", [SEQ_PER_CORE, SEQ, D], F32, kind="ExternalOutput")
    x2_dram = nc.dram_tensor("x2scratch", [SEQ_PER_CORE, SEQ, D], F32,
                             kind="Internal")
    wbf_dram = {nm: nc.dram_tensor(f"wbf_{nm}", [128, 8 * D], BF16, kind="Internal")
                for nm in ("Wq", "Wk")}
    wbf_dram["Wv"] = nc.dram_tensor("wbf_Wv", [D, D], BF16, kind="Internal")
    wbf_dram["Wo"] = nc.dram_tensor("wbf_Wo", [D, D], BF16, kind="Internal")
    wbf_dram["w1"] = nc.dram_tensor("wbf_w1", [D, FFN], BF16, kind="Internal")
    wbf_dram["w2"] = nc.dram_tensor("wbf_w2", [FFN, D], BF16, kind="Internal")
    twbfd = {nm: Buf(f"wbf_{nm}") for nm in ("Wq", "Wk", "Wv", "Wo", "w1", "w2")}
    tx2d = Buf("x2_dram")
    tap_h = {}
    _tapspec = {"ln1": ([128, NSUB * D], BF16), "lnT": ([128, 8 * MACRO], BF16),
                "q": ([128, H * MACRO], BF16), "k": ([128, H * MACRO], F32),
                "khat": ([128, H * MACRO], BF16), "lam": ([128, H * MACRO], F32),
                "Lam": ([128, H * MACRO], F32), "sig": ([128, H * MACRO], BF16),
                "v": ([64, NCH * D], BF16), "gate": ([128, NSUB * D], BF16),
                "y": ([128, NSUB * D], F32), "x2": ([128, NSUB * D], F32),
                "S0": ([128, 16 * HD], F32), "U": ([128, 1024], BF16),
                "Tm": ([128, 512], BF16), "Bm": ([128, 1024], BF16),
                "Am": ([128, 1024], BF16)}
    for tname in taps:
        shp, dt_ = _tapspec[tname]
        tap_h[tname] = nc.dram_tensor("tap_" + tname, shp, dt_,
                                      kind="ExternalOutput")

    mT, mTt, mW, mHQ, mKQ = _masks()
    c_maskT = nc.inline_tensor(np.tile(mT, (1, 4)).astype(BF), name="c_maskT")
    c_maskTt = nc.inline_tensor(np.tile(mTt, (1, 4)).astype(BF), name="c_maskTt")
    c_maskW = nc.inline_tensor(np.tile(mW, (1, 4)).astype(BF), name="c_maskW")
    c_maskHQ = nc.inline_tensor(np.tile(mHQ, (1, 8)).astype(BF), name="c_maskHQ")
    c_maskKQ = nc.inline_tensor(np.tile(mKQ, (1, 8)).astype(BF), name="c_maskKQ")
    c_ident = nc.inline_tensor(np.eye(128, dtype=BF), name="c_ident")
    c_ident32 = nc.inline_tensor(np.eye(128, dtype=np.float32), name="c_ident32")
    c_ones_col = nc.inline_tensor(np.ones((128, 1), dtype=BF), name="c_ones_col")
    c_ones_row = nc.inline_tensor(np.ones((1, 128), dtype=np.float32),
                                  name="c_ones_row")
    bc8 = np.zeros((8, 8 * 128), dtype=BF)
    for hh in range(8):
        bc8[hh, hh * 128:(hh + 1) * 128] = 1.0
    c_bcast8 = nc.inline_tensor(bc8, name="c_bcast8")

    b = Bld(nc)

    with ExitStack() as ctx:
        b.enter(ctx)

        def mk(ctx_, name, shape, dt=F32):
            import os as _os
            if _os.environ.get("KDBG_SBUF"):
                print(f"  alloc {name} {shape} {dt}: remaining {nc.sbuf_bytes_remaining}")
            t = ctx_.enter_context(nc.sbuf_tensor(name, shape, dt))
            return t, Buf(name)

        wstage, twstage = mk(ctx, "wstage", [128, 1024], F32)
        consts, tconsts = mk(ctx, "consts", [128, 1664], BF16)
        consts2, tconsts2 = mk(ctx, "consts2", [64, 768], BF16)
        ident32, tid32 = mk(ctx, "ident32", [128, 128], F32)
        onescol, tones = mk(ctx, "onescol", [128, 1], BF16)
        onesrow, tonesr = mk(ctx, "onesrow", [1, 128], F32)
        bcast8, tbc8 = mk(ctx, "bcast8", [8, 1024], BF16)
        xP = [mk(ctx, "x_sb0", [128, NSUB * D], F32)]
        lnTP = [mk(ctx, "lnT0", [128, 8 * MACRO], BF16)]
        yTP = [mk(ctx, "yT_bf0", [128, NSUB * D], BF16)]

        y_sb, ty = mk(ctx, "y_sb", [128, NSUB * D], F32)
        lncol, tlncol = mk(ctx, "lncol", [128, 16], F32)
        epsc, tepsc = mk(ctx, "epsc", [128, 4], F32)
        PS = []
        for i in range(8):
            t = ctx.enter_context(nc.psum_tensor(f"PB{i}", [128, 512], F32))
            PS.append((t, Buf(f"PB{i}")))

        def dma(dst_ap, src_ap, reads=(), writes=(), q="Y", dsem="DX"):
            return b.op(q, "dma_start", dst_ap, src_ap, reads=reads,
                        writes=writes, dma=True, dsem=dsem)

        IDB = consts[:, 1536:1664]

        def pe_transpose_8(src_fn, dst_fn, tsrc, tdst, pbank, tpbank, n=8,
                           rows=128, dst_batched=None, evac_eng="V"):
            for i in range(n):
                b.op("T", "transpose",
                     pbank[:].bitcast(BF16)[0:rows, i * 128:(i + 1) * 128]
                     if rows != 128 else
                     pbank[:].bitcast(BF16)[:, i * 128:(i + 1) * 128],
                     src_fn(i), IDB,
                     reads=[tsrc, tconsts], writes=[tpbank], inc=(i == n - 1))
            if dst_batched is not None:
                out_ap, in_ap = dst_batched
                b.op(evac_eng, "tensor_copy" if evac_eng == "V" else "copy",
                     out_ap, in_ap, reads=[tpbank], writes=[tdst])
                return
            for i in range(n):
                b.op("V", "tensor_copy", dst_fn(i),
                     pbank[:].bitcast(BF16)[0:rows, i * 128:(i + 1) * 128]
                     if rows != 128 else
                     pbank[:].bitcast(BF16)[:, i * 128:(i + 1) * 128],
                     reads=[tpbank], writes=[tdst], inc=(i == n - 1))

        def layernorm(xs, dst, tdst_, tx_, sq_tile, tsq_):
            b.op("V", "reduce_sum", lncol[:, 0:1], xs, axis=AX,
                 reads=[tx_], writes=[tlncol])
            b.op("V", "tensor_scalar_mul", lncol[:, 1:2], lncol[:, 0:1], 1.0 / D,
                 reads=[tlncol], writes=[tlncol])
            b.op("S", "activation", sq_tile, xs, AF.Square,
                 accum_out=lncol[:, 2:3], reads=[tx_], writes=[tsq_, tlncol])
            b.op("V", "tensor_scalar_mul", lncol[:, 3:4], lncol[:, 2:3], 1.0 / D,
                 reads=[tlncol], writes=[tlncol])
            b.op("V", "tensor_tensor", lncol[:, 4:5], lncol[:, 1:2], lncol[:, 1:2],
                 ALU.mult, reads=[tlncol], writes=[tlncol])
            b.op("V", "tensor_tensor", lncol[:, 5:6], lncol[:, 3:4], lncol[:, 4:5],
                 ALU.subtract, reads=[tlncol], writes=[tlncol])
            b.op("S", "activation", lncol[:, 6:7], lncol[:, 5:6], AF.Ln,
                 bias=epsc[:, 0:1], reads=[tlncol, tepsc], writes=[tlncol])
            b.op("S", "activation", lncol[:, 7:8], lncol[:, 6:7], AF.Exp,
                 scale=-0.5, reads=[tlncol], writes=[tlncol])
            b.op("V", "tensor_scalar", dst, xs, lncol[:, 1:2], lncol[:, 7:8],
                 ALU.subtract, ALU.mult, reads=[tx_, tlncol], writes=[tdst_])

        def _perm_in(pbank):
            return pbank[:].bitcast(BF16).rearrange(
                "p (s k e) -> p k s e", s=2, k=4)

        def _perm_out(dst, grp):
            return dst[:, grp * 1024:(grp + 1) * 1024].rearrange(
                "p (k s e) -> p k s e", k=4, s=2)

        def ln_and_transpose(x_sb, tx, lnT, tlnT, yT_bf, tyT):
            for sub in range(NSUB):
                layernorm(x_sb[:, sub * D:(sub + 1) * D],
                          yT_bf[:, sub * D:(sub + 1) * D], tyT, tx,
                          yT_bf[:, sub * D:(sub + 1) * D], tyT)
            for grp in range(2):
                pbank, tpbank = PS[7]
                idx = [(grp * 4 + i % 4, i // 4) for i in range(8)]
                pe_transpose_8(
                    lambda i: yT_bf[:, idx[i][1] * D + idx[i][0] * 128:
                                    idx[i][1] * D + (idx[i][0] + 1) * 128],
                    None, tyT, tlnT, pbank, tpbank,
                    dst_batched=(_perm_out(lnT, grp), _perm_in(pbank)))

        tapchain = Buf("tapchain")
        toutc = Buf("outchain")
        def tap(name, src_ap, reads):
            if name in tap_h:
                b.op("Y", "dma_start", tap_h[name].ap(), src_ap,
                     reads=list(reads), writes=[tapchain], dma=True, dsem="DT2")

        with ExitStack() as ctxA:
            scrA, tscrA = mk(ctxA, "scrA", [128, 2048], F32)
            xP.append(mk(ctxA, "x_sb1", [128, NSUB * D], F32))
            lnTP.append(mk(ctxA, "lnT1", [128, 8 * MACRO], BF16))
            yTP.append(mk(ctxA, "yT_bf1", [128, NSUB * D], BF16))
            Wbuf = [(ctxA.enter_context(nc.sbuf_tensor("Wbuf0", [128, 8 * D], BF16)),
                     [Buf(f"Wbuf0.{i}") for i in range(8)]),
                    (ctxA.enter_context(nc.sbuf_tensor("Wbuf1", [128, 8 * D], BF16)),
                     [Buf(f"Wbuf1.{i}") for i in range(8)])]
            Wf1_bf, tWf1 = mk(ctxA, "Wf1_bf", [128, 8 * HD], BF16)
            Wf2_bf, tWf2 = mk(ctxA, "Wf2_bf", [128, D], BF16)
            Wog1_bf, tWog1 = mk(ctxA, "Wog1_bf", [128, 8 * HD], BF16)
            Wog2_bf, tWog2 = mk(ctxA, "Wog2_bf", [128, D], BF16)
            Wbe_bf, tWbe = mk(ctxA, "Wbe_bf", [128, 8 * H], BF16)
            nw_col, tnw = mk(ctxA, "nw_col", [128, 8], F32)
            S0_f32, tS0 = mk(ctxA, "S0_f32", [128, 16 * HD], F32)
            S0_bf, tS0b = mk(ctxA, "S0_bf", [128, 16 * HD], BF16)
            q_bf, tq = mk(ctxA, "q_bf", [128, H * MACRO], BF16)
            k_f32, tk = mk(ctxA, "k_f32", [128, H * MACRO], F32)
            khat, tkh = mk(ctxA, "khat", [128, H * MACRO], BF16)
            lam, tlam = mk(ctxA, "lam", [128, H * MACRO], F32)
            Lam, tLam = mk(ctxA, "Lam", [128, H * MACRO], F32)
            sigbc, tsig = mk(ctxA, "sigbc", [128, H * MACRO], BF16)
            gate_bf, tgate = mk(ctxA, "gate_bf", [128, NSUB * D], BF16)
            v_bf, tv = mk(ctxA, "v_bf", [64, NCH * D], BF16)
            fh_bf, tfh = mk(ctxA, "fh_bf", [128, MACRO], BF16)
            ogh_bf, togh = mk(ctxA, "ogh_bf", [128, MACRO], BF16)
            bsig, tbsig = mk(ctxA, "bsig", [8, MACRO], BF16)
            sqrow, tsqrow = mk(ctxA, "sqrow", [1, 512], F32)
            sqbf, tsqbf = mk(ctxA, "sqbf", [1, 512], BF16)
            sqrowB, tsqrow2 = mk(ctxA, "sqrowB", [1, 512], F32)
            sqbfB, tsqbf2 = mk(ctxA, "sqbfB", [1, 512], BF16)
            rmscol, trms = mk(ctxA, "rmscol", [128, 16], F32)
            eX, teX = mk(ctxA, "eX", [128, 512], F32)
            lexcP = [mk(ctxA, "lexc0", [128, 512], F32),
                     mk(ctxA, "lexc1", [128, 512], F32)]
            t1s, tt1 = mk(ctxA, "t1s", [128, 512], F32)
            BmP = [mk(ctxA, "Bm0", [128, 1024], BF16),
                   mk(ctxA, "Bm1", [128, 1024], BF16)]
            AmP = [mk(ctxA, "Am0", [128, 1024], BF16),
                   mk(ctxA, "Am1", [128, 1024], BF16)]
            KtP = [mk(ctxA, "Kt0", [128, 512], BF16),
                   mk(ctxA, "Kt1", [128, 512], BF16)]
            qtP = [mk(ctxA, "qt0", [128, 512], BF16),
                   mk(ctxA, "qt1", [128, 512], BF16)]
            AtmP = [mk(ctxA, "Atm0", [128, 1024], BF16),
                    mk(ctxA, "Atm1", [128, 1024], BF16)]
            KtmP = [mk(ctxA, "Ktm0", [64, 1024], BF16),
                    mk(ctxA, "Ktm1", [64, 1024], BF16)]
            T_sb = [mk(ctxA, "T_sb0", [128, 512], BF16),
                    mk(ctxA, "T_sb1", [128, 512], BF16)]
            Pm_sb = [mk(ctxA, "Pm_sb0", [128, 512], BF16),
                     mk(ctxA, "Pm_sb1", [128, 512], BF16)]
            X_sb = [mk(ctxA, "X_sb0", [128, 512], BF16),
                    mk(ctxA, "X_sb1", [128, 512], BF16)]
            W_sb, tWsb = mk(ctxA, "W_sb", [64, 1024], BF16)
            HQ_sb, tHQ = mk(ctxA, "HQ_sb", [128, 512], BF16)
            KQ_sb, tKQ = mk(ctxA, "KQ_sb", [64, 512], BF16)
            U_sb, tU = mk(ctxA, "U_sb", [128, 1024], BF16)
            tUg = [tU, Buf("U_sb_g1")]
            egcP = [mk(ctxA, "egc0", [128, 8], F32),
                    mk(ctxA, "egc1", [128, 8], F32)]
            Ssc, tSsc = scrA[:, 1024:1536], tscrA

            blk = nc.Block()
            blk.__enter__()
            if True:
                dma(consts[:, 0:512], c_maskT.ap(), writes=[tconsts], dsem="DC")
                dma(consts[:, 512:1024], c_maskTt.ap(), writes=[tconsts], dsem="DC")
                dma(consts[:, 1024:1536], c_maskHQ.ap(), writes=[tconsts], dsem="DC")
                dma(consts[:, 1536:1664], c_ident.ap(), writes=[tconsts], dsem="DC")
                dma(consts2[:, 0:512], c_maskW.ap(), writes=[tconsts2], dsem="DC")
                dma(consts2[:, 512:768], c_maskKQ.ap()[:, 0:256], writes=[tconsts2], dsem="DC")
                dma(onescol[:], c_ones_col.ap(), writes=[tones], dsem="DC")
                dma(onesrow[:], c_ones_row.ap(), writes=[tonesr], dsem="DC")
                dma(bcast8[:], c_bcast8.ap(), writes=[tbc8], dsem="DC")
                dma(ident32[:], c_ident32.ap(), writes=[tid32], dsem="DC")
                dma(wstage[0:1, 0:1024], wh["norm_w"].ap()[None, :],
                    writes=[twstage], dsem="DW0")
                b.settle("DC", [tconsts, tconsts2, tones, tonesr, tbc8, tid32])
                b.op("V", "memset", epsc[:, 0:1], EPS_LN, writes=[tepsc])
                b.op("V", "memset", epsc[:, 1:2], EPS_K, writes=[tepsc])
                b.op("V", "memset", epsc[:, 2:3], EPS_RMS, writes=[tepsc])
                b.op("V", "memset", epsc[:, 3:4], 1.0, writes=[tepsc])
                b.op("V", "memset", S0_f32[:], 0.0, writes=[tS0])
                b.op("V", "memset", S0_bf[:], 0.0, writes=[tS0b])
                pb, tpb = PS[0]
                for kk in range(8):
                    b.op("T", "transpose", pb[:, kk:kk + 1],
                         wstage[0:1, kk * 128:(kk + 1) * 128], ident32[0:1, 0:1],
                         reads=[twstage, tid32], writes=[tpb], inc=(kk == 7))
                b.op("S", "copy", nw_col[:], pb[:, 0:8], reads=[tpb], writes=[tnw])

                _wl_par = [0]
                def load_weight(dst, tdst, src_h, ktiles, width, scale=False):
                    tl = tdst if isinstance(tdst, list) else [tdst] * ktiles
                    for kk in range(ktiles):
                        par = _wl_par[0] % 2
                        _wl_par[0] += 1
                        stg = (wstage[:, 0:width] if par == 0
                               else scrA[:, 0:width].bitcast(F32) if False
                               else scrA[:, 0:width])
                        tstg = twstage if par == 0 else tscrA
                        dma(stg, src_h.ap()[kk * 128:(kk + 1) * 128, :],
                            writes=[tstg], dsem=("DW0" if par == 0 else "DW1"))
                        if scale:
                            b.op("V", "tensor_scalar_mul",
                                 dst[:, kk * width:(kk + 1) * width], stg,
                                 nw_col[:, kk:kk + 1],
                                 reads=[tstg, tnw], writes=[tl[kk]])
                        else:
                            b.op("V", "tensor_copy",
                                 dst[:, kk * width:(kk + 1) * width], stg,
                                 reads=[tstg], writes=[tl[kk]])
                _wload_done = [False]
                def _do_weight_loads():
                    if _wload_done[0]:
                        return
                    _wload_done[0] = True
                    _weight_loads()
                def _weight_loads():
                    tbgchain = Buf("bgchain")
                    b.op("P", "dma_start", wbf_dram["Wv"].ap(), wh["Wv"].ap(),
                         writes=[twbfd["Wv"], tbgchain], dma=True, dsem="DB")
                    b.op("P", "dma_start", wbf_dram["Wo"].ap(), wh["Wo"].ap(),
                         writes=[twbfd["Wo"], tbgchain], dma=True, dsem="DB")
                    b.op("P", "dma_start", wbf_dram["w1"].ap(), wh["w1"].ap(),
                         writes=[twbfd["w1"], tbgchain], dma=True, dsem="DB")
                    b.op("P", "dma_start", wbf_dram["w2"].ap(), wh["w2"].ap(),
                         writes=[twbfd["w2"], tbgchain], dma=True, dsem="DB")
                    for nm, (wcast, twcast) in (("Wq", Wbuf[0]), ("Wk", Wbuf[1])):
                        for kk in range(8):
                            par = kk % 2
                            stg = wstage[:, 0:D] if par == 0 else scrA[:, 0:D]
                            tstg = twstage if par == 0 else tscrA
                            dma(stg, wh[nm].ap()[kk * 128:(kk + 1) * 128, :],
                                writes=[tstg], dsem=("DW0" if par == 0 else "DW1"))
                            b.op("V", "tensor_copy",
                                 wcast[:, kk * D:(kk + 1) * D], stg,
                                 reads=[tstg], writes=[twcast[kk]])
                        dma(wbf_dram[nm].ap(), wcast[:], reads=list(twcast),
                            writes=[twbfd[nm], tx2d], dsem="DSP")
                    load_weight(Wf1_bf, tWf1, wh["Wf1"], 8, HD)
                    load_weight(Wf2_bf, tWf2, wh["Wf2"], 1, D)
                    load_weight(Wog1_bf, tWog1, wh["Wog1"], 8, HD)
                    load_weight(Wog2_bf, tWog2, wh["Wog2"], 1, D)
                    load_weight(Wbe_bf, tWbe, wh["Wbeta"], 8, H)

            def chunk3(tn, cc):
                r = tn[:].rearrange("p (h t) -> p h t", h=H)
                return r[:, :, cc * C:(cc + 1) * C]

            def bl3(tn):
                ap = tn[:] if hasattr(tn, "shape") and not isinstance(tn, bass.AP) else tn
                try:
                    ap = tn[:]
                except TypeError:
                    ap = tn
                return ap.rearrange("p (h t) -> p h t", h=H)

            if True:
                def emit_preA(mj):
                    if mj >= n_macros:
                        return
                    s2 = mj % SEQ_PER_CORE
                    t02 = (mj // SEQ_PER_CORE) * MACRO
                    x2b, tx2b = xP[mj % 2]
                    lnT2, tlnT2 = lnTP[mj % 2]
                    yT2, tyT2 = yTP[mj % 2]
                    for sub in range(NSUB):
                        dma(x2b[:, sub * D:(sub + 1) * D],
                            x_h.ap()[s2, t02 + sub * 128: t02 + (sub + 1) * 128, :],
                            writes=[tx2b], dsem=("DX" if mj % 2 == 0 else "DX1"))
                    ln_and_transpose(x2b, tx2b, lnT2, tlnT2, yT2, tyT2)
                def emit_qproj(mj):
                    if mj >= n_macros:
                        return
                    lnT2, tlnT2 = lnTP[mj % 2]
                    for hh in range(H):
                        bank, tb = PS[hh % 4]
                        for kk in range(8):
                            b.op("T", "matmul", bank[:, 0:MACRO],
                                 Wbuf[0][0][:, kk * D + hh * HD:
                                             kk * D + (hh + 1) * HD],
                                 lnT2[:, kk * MACRO:(kk + 1) * MACRO],
                                 start=(kk == 0), stop=(kk == 7),
                                 reads=[Wbuf[0][1][kk], tlnT2], writes=[tb],
                                 inc=(kk == 7))
                        b.op("S", "activation",
                             q_bf[:, hh * MACRO:(hh + 1) * MACRO],
                             bank[:, 0:MACRO], AF.Silu,
                             reads=[tb], writes=[tq])
                    for kk in range(8):
                        dma(Wbuf[0][0][:, kk * D:(kk + 1) * D],
                            wbf_dram["Wv"].ap()[kk * 128:(kk + 1) * 128, :],
                            reads=[twbfd["Wv"]], writes=[Wbuf[0][1][kk]],
                            dsem=f"DS{kk}")
                emit_preA(0)
                _do_weight_loads()
                emit_qproj(0)
                for mi in range(n_macros):
                    s = mi % SEQ_PER_CORE
                    mb = mi // SEQ_PER_CORE
                    t0 = mb * MACRO
                    last_macro = mi == n_macros - 1
                    x_sb, tx = xP[mi % 2]
                    lnT, tlnT = lnTP[mi % 2]
                    yT_bf, tyT = yTP[mi % 2]
                    if last_macro:
                        tap("ln1", yT_bf[:], [tyT])
                        tap("lnT", lnT[:], [tlnT])
                    for dst, tdst, (Wt, tWt) in ((k_f32, tk, Wbuf[1]),):
                        for hh in range(H):
                            bank, tb = PS[hh % 4]
                            for kk in range(8):
                                b.op("T", "matmul", bank[:, 0:MACRO],
                                     Wt[:, kk * D + hh * HD: kk * D + (hh + 1) * HD],
                                     lnT[:, kk * MACRO:(kk + 1) * MACRO],
                                     start=(kk == 0), stop=(kk == 7),
                                     reads=[tWt[kk], tlnT], writes=[tb], inc=(kk == 7))
                            b.op("S", "activation",
                                 dst[:, hh * MACRO:(hh + 1) * MACRO],
                                 bank[:, 0:MACRO], AF.Silu,
                                 reads=[tb], writes=[tdst])
                    for kk in range(8):
                        dma(Wbuf[1][0][:, kk * D:(kk + 1) * D],
                            wbf_dram["Wo"].ap()[kk * 128:(kk + 1) * 128, :],
                            reads=[twbfd["Wo"]], writes=[Wbuf[1][1][kk]],
                            dsem=f"DU{kk}")
                    for dst, tdst, Wt, tWt, bankno in ((fh_bf, tfh, Wf1_bf, tWf1, 4),
                                                       (ogh_bf, togh, Wog1_bf, tWog1, 5)):
                        bank, tb = PS[bankno]
                        for kk in range(8):
                            b.op("T", "matmul", bank[:, 0:MACRO],
                                 Wt[:, kk * HD:(kk + 1) * HD],
                                 lnT[:, kk * MACRO:(kk + 1) * MACRO],
                                 start=(kk == 0), stop=(kk == 7),
                                 reads=[tWt, tlnT], writes=[tb], inc=(kk == 7))
                        b.op("V", "tensor_copy", dst[:], bank[:, 0:MACRO],
                             reads=[tb], writes=[tdst])
                    bank, tb = PS[6]
                    for kk in range(8):
                        b.op("T", "matmul", bank[0:8, 0:MACRO],
                             Wbe_bf[:, kk * H:(kk + 1) * H],
                             lnT[:, kk * MACRO:(kk + 1) * MACRO],
                             start=(kk == 0), stop=(kk == 7),
                             reads=[tWbe, tlnT], writes=[tb], inc=(kk == 7))
                    b.op("S", "activation", bsig[:], bank[0:8, 0:MACRO], AF.Sigmoid,
                         reads=[tb], writes=[tbsig])
                    for hp in range(4):
                        bank, tb = PS[hp % 2]
                        for j in range(2):
                            hh = hp * 2 + j
                            b.op("T", "matmul", bank[:, j * MACRO:(j + 1) * MACRO],
                                 Wf2_bf[:, hh * HD:(hh + 1) * HD], fh_bf[:],
                                 start=True, stop=True,
                                 reads=[tWf2, tfh], writes=[tb], inc=(j == 1))
                        b.op("S", "activation", scrA[:, 0:512], bank[:], AF.Exp,
                             scale=-1.0, reads=[tb], writes=[tscrA])
                        b.op("S", "activation", lam[:, hp * 512:(hp + 1) * 512],
                             scrA[:, 0:512], AF.Ln, bias=epsc[:, 3:4],
                             reads=[tscrA, tepsc], writes=[tlam])
                    for j in range(4):
                        b.op("S", "activation", KtP[j % 2][0][:],
                             k_f32[:, j * 512:(j + 1) * 512], AF.Square,
                             reads=[tk], writes=[KtP[j % 2][1]])
                        bank, tb = PS[4 + j % 2]
                        b.op("T", "matmul", bank[0:1, :], onescol[:],
                             KtP[j % 2][0][:], start=True, stop=True,
                             reads=[tones, KtP[j % 2][1]], writes=[tb])
                        srow = sqrow if j % 2 == 0 else sqrowB
                        tsrow = tsqrow if j % 2 == 0 else tsqrow2
                        sbfr = sqbf if j % 2 == 0 else sqbfB
                        tsbfr = tsqbf if j % 2 == 0 else tsqbf2
                        b.op("S", "activation", srow[0:1, :],
                             bank[0:1, :], AF.Ln, bias=epsc[0:1, 1:2],
                             reads=[tb, tepsc], writes=[tsrow])
                        b.op("S", "activation", sbfr[0:1, :],
                             srow[0:1, :], AF.Exp, scale=-0.5,
                             reads=[tsrow], writes=[tsbfr])
                        b.op("T", "matmul", bank[:], onescol[0:1, 0:1].broadcast_to((1, 128)),
                             sbfr[0:1, :], start=True, stop=True,
                             reads=[tones, tsbfr], writes=[tb])
                        b.op("V", "scalar_tensor_tensor",
                             khat[:, j * 512:(j + 1) * 512],
                             bank[:], 1.0,
                             k_f32[:, j * 512:(j + 1) * 512],
                             ALU.mult, ALU.mult,
                             reads=[tb, tk], writes=[tkh])
                    for hh in range(H):
                        bank, tb = PS[6 + hh % 2]
                        b.op("T", "matmul", bank[:, 0:MACRO],
                             bcast8[:, hh * 128:(hh + 1) * 128], bsig[:],
                             start=True, stop=True,
                             reads=[tbc8, tbsig], writes=[tb])
                        b.op("S", "copy", sigbc[:, hh * MACRO:(hh + 1) * MACRO],
                             bank[:, 0:MACRO], reads=[tb], writes=[tsig])
                    lam4 = lam[:].rearrange("p (h c t) -> p h c t", h=H, c=NCH)
                    Lam4 = Lam[:].rearrange("p (h c t) -> p h c t", h=H, c=NCH)
                    for hh in range(H):
                        for cc in range(NCH):
                            b.op("V", "tensor_tensor_scan",
                                 Lam4[:, hh, cc, :], lam4[:, hh, cc, :],
                                 lam4[:, hh, cc, :], 0.0, ALU.add, ALU.bypass,
                                 reads=[tlam], writes=[tLam])
                    _beng = ["P"]
                    def chunk_build_elem(cc):
                        sub, half = cc // 2, cc % 2
                        Bm, tBm = BmP[cc % 2]
                        Am, tAm = AmP[cc % 2]
                        Kt, tKt = KtP[cc % 2]
                        qt, tqt = qtP[cc % 2]
                        Atm, tAtm = AtmP[cc % 2]
                        Ktm, tKtm = KtmP[cc % 2]
                        egc, tegc = egcP[cc % 2]
                        lexc, tlexc = lexcP[cc % 2]
                        b.op(_beng[0], "tensor_tensor", bl3(lexc), chunk3(Lam, cc),
                             chunk3(lam, cc), ALU.subtract,
                             reads=[tLam, tlam], writes=[tlexc])
                        BmR = Bm[:].rearrange("p (h t) -> p h t", h=H)
                        AmR = Am[:].rearrange("p (h t) -> p h t", h=H)
                        b.op("S", "activation", bl3(eX), bl3(lexc), AF.Exp,
                             scale=-1.0, reads=[tlexc], writes=[teX])
                        b.op(_beng[0], "tensor_tensor", BmR[:, :, 0:C], chunk3(khat, cc),
                             bl3(eX), ALU.mult, reads=[tkh, teX], writes=[tBm])
                        b.op("S", "activation", bl3(eX), chunk3(Lam, cc), AF.Exp,
                             scale=-1.0, reads=[tLam], writes=[teX])
                        b.op(_beng[0], "tensor_tensor", BmR[:, :, C:L], chunk3(khat, cc),
                             bl3(eX), ALU.mult, reads=[tkh, teX], writes=[tBm])
                        b.op(_beng[0], "tensor_tensor", bl3(qt), chunk3(q_bf, cc),
                             bl3(eX), ALU.mult, reads=[tq, teX], writes=[tqt])
                        b.op("S", "activation", bl3(eX), bl3(lexc), AF.Exp,
                             reads=[tlexc], writes=[teX])
                        b.op(_beng[0], "tensor_tensor", bl3(t1s), chunk3(khat, cc),
                             bl3(eX), ALU.mult, reads=[tkh, teX], writes=[tt1])
                        b.op("S", "activation", bl3(eX), chunk3(Lam, cc), AF.Exp,
                             reads=[tLam], writes=[teX])
                        b.op(_beng[0], "tensor_tensor", bl3(Kt), chunk3(khat, cc),
                             bl3(eX), ALU.mult, reads=[tkh, teX], writes=[tKt])
                        b.op("V", "scalar_tensor_tensor", AmR[:, :, 0:C],
                             chunk3(sigbc, cc), -2.0, bl3(t1s), ALU.mult, ALU.mult,
                             reads=[tsig, tt1], writes=[tAm])
                        b.op("V", "scalar_tensor_tensor", AmR[:, :, C:L],
                             chunk3(sigbc, cc), -2.0, bl3(Kt), ALU.mult, ALU.mult,
                             reads=[tsig, tKt], writes=[tAm])
                        b.op(_beng[0], "tensor_copy", egc[:, 0:8],
                             Lam4[:, :, cc, C - 1], reads=[tLam], writes=[tegc])
                        b.op("S", "activation", egc[:, 0:8], egc[:, 0:8], AF.Exp,
                             scale=-1.0, reads=[tegc], writes=[tegc])
                        if last_macro and cc == 0:
                            tap("Bm", Bm[:], [tBm]); tap("Am", Am[:], [tAm])

                    def chunk_build_tr(cc):
                        sub, half = cc // 2, cc % 2
                        Bm, tBm = BmP[cc % 2]
                        Am, tAm = AmP[cc % 2]
                        Kt, tKt = KtP[cc % 2]
                        qt, tqt = qtP[cc % 2]
                        Atm, tAtm = AtmP[cc % 2]
                        Ktm, tKtm = KtmP[cc % 2]
                        egc, tegc = egcP[cc % 2]
                        lexc, tlexc = lexcP[cc % 2]
                        pb, tpb = PS[6]
                        pe_transpose_8(
                            lambda i: Am[:, i * 128:(i + 1) * 128],
                            lambda i: Atm[:, i * 128:(i + 1) * 128],
                            tAm, tAtm, pb, tpb)
                        pb2, tpb2 = PS[7]
                        pe_transpose_8(
                            lambda i: Kt[:, i * 64:(i + 1) * 64],
                            lambda i: Ktm[:, i * 128:(i + 1) * 128],
                            tKt, tKtm, pb2, tpb2, rows=64)

                    def chunk_build_ops(cc):
                        ops = []
                        real_op = b.op
                        def rec(*a, **k):
                            ops.append((a, k))
                        b.op = rec
                        try:
                            chunk_build_elem(cc)
                            chunk_build_tr(cc)
                        finally:
                            b.op = real_op
                        return ops

                    _beng[0] = "V"
                    chunk_build_elem(0)
                    _beng[0] = "P"
                    for sub in range(NSUB):
                        for j in range(2):
                            bank, tb = PS[2 + 2 * sub + j]
                            b.op("T", "matmul", bank[:],
                                 ogh_bf[:, sub * 128:(sub + 1) * 128],
                                 Wog2_bf[:, j * 512:(j + 1) * 512],
                                 start=True, stop=True,
                                 reads=[togh, tWog2], writes=[tb])
                            b.op("S", "activation",
                                 gate_bf[:, sub * D + j * 512: sub * D + (j + 1) * 512],
                                 bank[:], AF.Sigmoid, reads=[tb], writes=[tgate])
                    for sub in range(NSUB):
                        for j in range(2):
                            bank, tb = PS[(sub * 2 + j) % 4]
                            for kk in range(8):
                                b.op("T", "matmul", bank[:],
                                     lnT[:, kk * MACRO + sub * 128:
                                         kk * MACRO + (sub + 1) * 128],
                                     Wbuf[0][0][:, kk * D + j * 512:
                                                 kk * D + (j + 1) * 512],
                                     start=(kk == 0), stop=(kk == 7),
                                     reads=[tlnT, Wbuf[0][1][kk]], writes=[tb],
                                     inc=(kk == 7))
                            for half in range(2):
                                ch = 2 * sub + half
                                b.op("S", "activation",
                                     v_bf[:, ch * D + j * 512: ch * D + (j + 1) * 512],
                                     bank[half * 64:(half + 1) * 64, :], AF.Silu,
                                     reads=[tb], writes=[tv])
                    chunk_build_tr(0)
                    if not last_macro:
                        dma(Wbuf[0][0][:], wbf_dram["Wq"].ap(),
                            reads=[twbfd["Wq"]], writes=list(Wbuf[0][1]),
                            dsem="DP0")
                    if last_macro:
                        tap("q", q_bf[:], [tq]); tap("k", k_f32[:], [tk])
                        tap("khat", khat[:], [tkh]); tap("lam", lam[:], [tlam])
                        tap("Lam", Lam[:], [tLam]); tap("sig", sigbc[:], [tsig])
                        tap("v", v_bf[:], [tv]); tap("gate", gate_bf[:], [tgate])
                    for cc in range(NCH):
                        if cc == 2:
                            emit_preA(mi + 1)
                        sub, half = cc // 2, cc % 2
                        Bm, tBm = BmP[cc % 2]
                        Am, tAm = AmP[cc % 2]
                        Kt, tKt = KtP[cc % 2]
                        qt, tqt = qtP[cc % 2]
                        Atm, tAtm = AtmP[cc % 2]
                        Ktm, tKtm = KtmP[cc % 2]
                        egc, tegc = egcP[cc % 2]
                        lexc, tlexc = lexcP[cc % 2]
                        GRP = [list(range(0, 4)), list(range(4, 8))]
                        XBs = [PS[2], PS[4]]
                        PBs = [PS[3], PS[5]]
                        WBs = [PS[6], PS[7]]
                        UBs = [PS[0], PS[1]]
                        for g in range(2):
                            XB, tXB = XBs[g]
                            PB, tPB = PBs[g]
                            for i, hh in enumerate(GRP[g]):
                                b.op("T", "matmul",
                                     XB[:, i * 128:(i + 1) * 128],
                                     Bm[:, hh * 128:(hh + 1) * 128],
                                     Am[:, hh * 128:(hh + 1) * 128],
                                     start=True, stop=True,
                                     reads=[tBm, tAm], writes=[tXB], inc=(i == 3))
                            for i, hh in enumerate(GRP[g]):
                                b.op("T", "matmul",
                                     PB[:, i * 128:(i + 1) * 128],
                                     Am[:, hh * 128:(hh + 1) * 128],
                                     Bm[:, hh * 128:(hh + 1) * 128],
                                     start=True, stop=True,
                                     reads=[tBm, tAm], writes=[tPB], inc=(i == 3))
                        for g in range(2):
                            b.op("V", "tensor_tensor", T_sb[g][0][:],
                                 XBs[g][0][:],
                                 consts[:, 0:512], ALU.mult,
                                 reads=[XBs[g][1], tconsts], writes=[T_sb[g][1]])
                            b.op("V", "tensor_tensor", Pm_sb[g][0][:],
                                 PBs[g][0][:],
                                 consts[:, 512:1024], ALU.mult,
                                 reads=[PBs[g][1], tconsts], writes=[Pm_sb[g][1]])
                        if last_macro and cc == 0:
                            tap("Tm", T_sb[0][0][:], [T_sb[0][1]])
                        for g in range(2):
                            WB, tWB = WBs[g]
                            for i, hh in enumerate(GRP[g]):
                                b.op("T", "matmul",
                                     WB[0:64, i * 128:(i + 1) * 128],
                                     Kt[:, hh * 64:(hh + 1) * 64],
                                     Bm[:, hh * 128:(hh + 1) * 128],
                                     start=True, stop=True,
                                     reads=[tKt, tBm], writes=[tWB], inc=(i == 3))
                        for g in range(2):
                            b.op("V", "tensor_tensor", W_sb[:, g * 512:(g + 1) * 512],
                                 WBs[g][0][0:64, :], consts2[:, 0:512], ALU.mult,
                                 reads=[WBs[g][1], tconsts2], writes=[tWsb])
                        for g in range(2):
                            HB, tHB = WBs[g]
                            for i, hh in enumerate(GRP[g]):
                                b.op("T", "matmul",
                                     HB[:, i * 64:(i + 1) * 64],
                                     Am[:, hh * 128:(hh + 1) * 128],
                                     qt[:, hh * 64:(hh + 1) * 64],
                                     start=True, stop=True,
                                     reads=[tAm, tqt], writes=[tHB], inc=(i == 3))
                            for i, hh in enumerate(GRP[g]):
                                b.op("T", "matmul",
                                     HB[0:64, 256 + i * 64: 256 + (i + 1) * 64],
                                     Kt[:, hh * 64:(hh + 1) * 64],
                                     qt[:, hh * 64:(hh + 1) * 64],
                                     start=True, stop=True,
                                     reads=[tKt, tqt], writes=[tHB], inc=(i == 3))
                        for g in range(2):
                            b.op("V", "tensor_tensor",
                                 HQ_sb[:, g * 256:(g + 1) * 256],
                                 WBs[g][0][:, 0:256],
                                 consts[:, 1024:1280], ALU.mult,
                                 reads=[WBs[g][1], tconsts], writes=[tHQ])
                            b.op("V", "tensor_tensor",
                                 KQ_sb[:, g * 256:(g + 1) * 256],
                                 WBs[g][0][0:64, 256:512],
                                 consts2[:, 512:768], ALU.mult,
                                 reads=[WBs[g][1], tconsts2], writes=[tKQ])
                        for g in range(2):
                            UB, tUB = UBs[g]
                            for i, hh in enumerate(GRP[g]):
                                sl = s * 8 + hh
                                b.op("T", "matmul",
                                     UB[:, i * 128:(i + 1) * 128],
                                     Bm[:, hh * 128:(hh + 1) * 128],
                                     S0_bf[:, sl * 128:(sl + 1) * 128],
                                     start=True, stop=False,
                                     reads=[tBm, tS0b], writes=[tUB], inc=False)
                                b.op("T", "matmul",
                                     UB[:, i * 128:(i + 1) * 128],
                                     W_sb[:, g * 512 + i * 128: g * 512 + (i + 1) * 128],
                                     v_bf[:, cc * D + hh * 128: cc * D + (hh + 1) * 128],
                                     start=False, stop=True,
                                     reads=[tWsb, tv], writes=[tUB], inc=(i == 3))
                        pending = (chunk_build_ops(cc + 1)
                                   if cc + 1 < NCH else [])
                        nper = (len(pending) + 6) // 7 if pending else 0
                        for lev in range(7):
                            for g in range(2):
                                b.op("S", "copy", U_sb[:, g * 512:(g + 1) * 512],
                                     UBs[g][0][:],
                                     reads=[UBs[g][1]], writes=[tUg[g]])
                            if lev < 5:
                                for g in range(2):
                                    XB, tXB = XBs[g]
                                    srcX, tsrcX = ((T_sb[g][0], T_sb[g][1]) if lev == 0
                                                   else (X_sb[g][0], X_sb[g][1]))
                                    for i in range(4):
                                        b.op("T", "matmul",
                                             XB[:, i * 128:(i + 1) * 128],
                                             Pm_sb[g][0][:, i * 128:(i + 1) * 128],
                                             srcX[:, i * 128:(i + 1) * 128],
                                             start=True, stop=True,
                                             reads=[Pm_sb[g][1], tsrcX], writes=[tXB],
                                             inc=(i == 3))
                            for g in range(2):
                                UB, tUB = UBs[g]
                                for i, hh in enumerate(GRP[g]):
                                    b.op("T", "matmul",
                                         UB[:, i * 128:(i + 1) * 128],
                                         IDB, U_sb[:, hh * 128:(hh + 1) * 128],
                                         start=True, stop=False,
                                         reads=[tconsts, tUg[g]], writes=[tUB], inc=False)
                                    b.op("T", "matmul",
                                         UB[:, i * 128:(i + 1) * 128],
                                         Pm_sb[g][0][:, i * 128:(i + 1) * 128],
                                         U_sb[:, hh * 128:(hh + 1) * 128],
                                         start=False, stop=True,
                                         reads=[Pm_sb[g][1], tUg[g]], writes=[tUB],
                                         inc=(i == 3))
                            if lev < 6:
                                for g in range(2):
                                    PB, tPB = PBs[g]
                                    srcX, tsrcX = ((T_sb[g][0], T_sb[g][1]) if lev == 0
                                                   else (X_sb[g][0], X_sb[g][1]))
                                    for i in range(4):
                                        b.op("T", "matmul",
                                             PB[:, i * 128:(i + 1) * 128],
                                             srcX[:, i * 128:(i + 1) * 128],
                                             Pm_sb[g][0][:, i * 128:(i + 1) * 128],
                                             start=True, stop=True,
                                             reads=[Pm_sb[g][1], tsrcX], writes=[tPB],
                                             inc=(i == 3))
                                for g in range(2):
                                    if lev < 5:
                                        b.op("V", "tensor_copy", X_sb[g][0][:],
                                             XBs[g][0][:],
                                             reads=[XBs[g][1]], writes=[X_sb[g][1]])
                                    b.op("V", "tensor_copy", Pm_sb[g][0][:],
                                         PBs[g][0][:],
                                         reads=[PBs[g][1]], writes=[Pm_sb[g][1]])
                            for _ in range(nper):
                                if pending:
                                    a_, k_ = pending.pop(0)
                                    b.op(*a_, **k_)
                        for g in range(2):
                            b.op("S", "copy", U_sb[:, g * 512:(g + 1) * 512],
                                 UBs[g][0][:],
                                 reads=[UBs[g][1]], writes=[tUg[g]])
                        while pending:
                            a_, k_ = pending.pop(0)
                            b.op(*a_, **k_)
                        if last_macro and cc == 0:
                            tap("U", U_sb[:], [tUg[0], tUg[1]])
                        for g in range(2):
                            OB, tOB = XBs[g]
                            for i, hh in enumerate(GRP[g]):
                                sl = s * 8 + hh
                                b.op("T", "matmul", OB[0:64, i * 128:(i + 1) * 128],
                                     qt[:, hh * 64:(hh + 1) * 64],
                                     S0_bf[:, sl * 128:(sl + 1) * 128],
                                     start=True, stop=False,
                                     reads=[tqt, tS0b], writes=[tOB], inc=False)
                                b.op("T", "matmul", OB[0:64, i * 128:(i + 1) * 128],
                                     HQ_sb[:, g * 256 + i * 64: g * 256 + (i + 1) * 64],
                                     U_sb[:, hh * 128:(hh + 1) * 128],
                                     start=False, stop=False,
                                     reads=[tHQ, tUg[g]], writes=[tOB], inc=False)
                                b.op("T", "matmul", OB[0:64, i * 128:(i + 1) * 128],
                                     KQ_sb[:, g * 256 + i * 64: g * 256 + (i + 1) * 64],
                                     v_bf[:, cc * D + hh * 128: cc * D + (hh + 1) * 128],
                                     start=False, stop=True,
                                     reads=[tKQ, tv], writes=[tOB], inc=(i == 3))
                        for g in range(2):
                            b.op("S", "copy",
                                 y_sb[half * 64:(half + 1) * 64,
                                      sub * D + g * 512: sub * D + (g + 1) * 512],
                                 XBs[g][0][0:64, :], reads=[XBs[g][1]], writes=[ty])
                        for g in range(2):
                            SB, tSB = PBs[g]
                            for i, hh in enumerate(GRP[g]):
                                b.op("T", "matmul", SB[:, i * 128:(i + 1) * 128],
                                     Atm[:, hh * 128:(hh + 1) * 128],
                                     U_sb[:, hh * 128:(hh + 1) * 128],
                                     start=True, stop=False,
                                     reads=[tAtm, tUg[g]], writes=[tSB], inc=False)
                                b.op("T", "matmul", SB[:, i * 128:(i + 1) * 128],
                                     Ktm[:, hh * 128:(hh + 1) * 128],
                                     v_bf[:, cc * D + hh * 128: cc * D + (hh + 1) * 128],
                                     start=False, stop=True,
                                     reads=[tKtm, tv], writes=[tSB], inc=(i == 3))
                        for g in range(2):
                            sl0 = s * 8 + 4 * g
                            b.op("V", "tensor_tensor", Ssc, PBs[g][0][:],
                                 S0_f32[:, sl0 * 128:(sl0 + 4) * 128], ALU.add,
                                 reads=[PBs[g][1], tS0], writes=[tSsc])
                            b.op("V", "tensor_tensor",
                                 S0_f32[:, sl0 * 128:(sl0 + 4) * 128].rearrange(
                                     "p (l d) -> p l d", l=4),
                                 Ssc.rearrange("p (l d) -> p l d", l=4),
                                 egc[:, 4 * g:4 * g + 4].rearrange(
                                     "p (l o) -> p l o", o=1).broadcast_to(
                                     (128, 4, 128)),
                                 ALU.mult, reads=[tSsc, tegc], writes=[tS0])
                            b.op("S", "copy", S0_bf[:, sl0 * 128:(sl0 + 4) * 128],
                                 S0_f32[:, sl0 * 128:(sl0 + 4) * 128],
                                 reads=[tS0], writes=[tS0b])

                    for j in range(2):
                        b.op("V", "tensor_tensor", scrA[:, j * 1024:(j + 1) * 1024],
                             y_sb[:, j * 1024:(j + 1) * 1024],
                             gate_bf[:, j * 1024:(j + 1) * 1024], ALU.mult,
                             reads=[ty, tgate], writes=[tscrA])
                    if last_macro:
                        tap("y", scrA[:, 0:2048], [tscrA])
                    for j in range(2):
                        b.op("S", "activation", y_sb[:, j * 1024:(j + 1) * 1024],
                             scrA[:, j * 1024:(j + 1) * 1024], AF.Square,
                             reads=[tscrA], writes=[ty])
                    for sub in range(NSUB):
                        b.op("V", "reduce_sum",
                             rmscol[:, sub * H:(sub + 1) * H].rearrange(
                                 "p (h o) -> p h o", o=1),
                             y_sb[:, sub * D:(sub + 1) * D].rearrange(
                                 "p (h d) -> p h d", h=H),
                             axis=AX, reads=[ty], writes=[trms])
                    b.op("S", "activation", rmscol[:], rmscol[:], AF.Ln,
                         scale=1.0 / HD, bias=epsc[:, 2:3], reads=[trms, tepsc],
                         writes=[trms])
                    b.op("S", "activation", rmscol[:], rmscol[:], AF.Exp,
                         scale=-0.5, reads=[trms], writes=[trms])
                    for sub in range(NSUB):
                        b.op("V", "tensor_tensor",
                             yT_bf[:, sub * D:(sub + 1) * D].rearrange(
                                 "p (h d) -> p h d", h=H),
                             scrA[:, sub * D:(sub + 1) * D].rearrange(
                                 "p (h d) -> p h d", h=H),
                             rmscol[:, sub * H:(sub + 1) * H].rearrange(
                                 "p (h o) -> p h o", o=1).broadcast_to((128, H, HD)),
                             ALU.mult, reads=[tscrA, trms], writes=[tyT])
                    emit_qproj(mi + 1)
                    for grp in range(2):
                        pbank, tpbank = PS[7]
                        idx = [(grp * 4 + i % 4, i // 4) for i in range(8)]
                        for i in range(8):
                            kk, sub = idx[i]
                            b.op("T", "transpose",
                                 pbank[:].bitcast(BF16)[:, i * 128:(i + 1) * 128],
                                 yT_bf[:, sub * D + kk * 128:
                                       sub * D + (kk + 1) * 128], IDB,
                                 reads=[tyT, tconsts], writes=[tpbank],
                                 inc=(i == 7))
                        nwb = bass.AP(nw_col.handle if hasattr(nw_col, "handle")
                                      else nw_col, 0, [[8, 128], [1, 8]]) \
                            if False else nw_col[:]
                        nw4 = nwb[:, grp * 4:(grp + 1) * 4]
                        nw_b = nw4.rearrange("p (k o) -> p k o", o=1)
                        nw_b = nw_b.unsqueeze(2).broadcast_to((128, 4, 2, 128)) \
                            if hasattr(nw_b, "unsqueeze") else nw_b
                        b.op("V", "tensor_tensor", _perm_out(BmP[grp][0], 0),
                             _perm_in(pbank), nw_b, ALU.mult,
                             reads=[tpbank, tnw], writes=[BmP[grp][1]])
                    for sub in range(NSUB):
                        for j in range(2):
                            bank, tb = PS[4 + sub * 2 + j]
                            for kk in range(8):
                                b.op("T", "matmul", bank[:],
                                     BmP[kk // 4][0][:, (kk % 4) * 256 + sub * 128:
                                                     (kk % 4) * 256 + (sub + 1) * 128],
                                     Wbuf[1][0][:, kk * D + j * 512:
                                                 kk * D + (j + 1) * 512],
                                     start=(kk == 0), stop=(kk == 7),
                                     reads=[BmP[kk // 4][1], Wbuf[1][1][kk]],
                                     writes=[tb], inc=(kk == 7))
                            b.op("V", "tensor_tensor",
                                 x_sb[:, sub * D + j * 512: sub * D + (j + 1) * 512],
                                 bank[:],
                                 x_sb[:, sub * D + j * 512: sub * D + (j + 1) * 512],
                                 ALU.add, reads=[tb, tx], writes=[tx])
                        dma(x2_dram.ap()[s, t0 + sub * 128: t0 + (sub + 1) * 128, :],
                            x_sb[:, sub * D:(sub + 1) * D], reads=[tx],
                            writes=[tx2d], dsem="DSP")
                    if not last_macro:
                        for kk in range(8):
                            dma(Wbuf[1][0][:, kk * D:(kk + 1) * D],
                                wbf_dram["Wk"].ap()[:, kk * D:(kk + 1) * D],
                                reads=[twbfd["Wk"]], writes=[Wbuf[1][1][kk]],
                                dsem=f"DK{kk}")
                    if last_macro:
                        tap("x2", x_sb[:], [tx])
                        tap("S0", S0_f32[:], [tS0])
                b.flush(blk)
            blk.__exit__(None, None, None)

        with ExitStack() as ctxB:
            W1_bf = ctxB.enter_context(nc.sbuf_tensor("W1_bf", [128, 8 * FFN], BF16))
            tW1 = [[Buf(f"W1.{k}.{p}") for p in range(4)] for k in range(8)]
            W2_bf = ctxB.enter_context(nc.sbuf_tensor("W2_bf", [128, 32 * D], BF16))
            tW2 = [Buf(f"W2.{k}") for k in range(32)]
            xPB = [xP[0], mk(ctxB, "x_sbB", [128, NSUB * D], F32)]
            lnTPB = [lnTP[0], mk(ctxB, "lnTB", [128, 8 * MACRO], BF16)]
            yTPB = [yTP[0], mk(ctxB, "yT_bfB", [128, NSUB * D], BF16)]
            b1c, tb1c = mk(ctxB, "b1c", [128, 32], F32)
            b2bc, tb2bc = mk(ctxB, "b2bc", [128, D], F32)
            hfm, thfm = mk(ctxB, "hfm", [128, 32 * MACRO], BF16)

            with nc.Block() as blk:
                for e in ENGS:
                    b.wait_everything(e)
                dma(wstage[0:32, 0:128],
                    wh["b1"].ap().rearrange("(a b) -> a b", a=32),
                    writes=[twstage], dsem="DW0")
                pb, tpb = PS[0]
                b.op("T", "transpose", pb[:, 0:32], wstage[0:32, 0:128],
                     ident32[0:32, 0:32], reads=[twstage, tid32], writes=[tpb])
                b.op("V", "tensor_copy", b1c[:], pb[:, 0:32], reads=[tpb],
                     writes=[tb1c])
                dma(wstage[0:1, 0:1024], wh["b2"].ap()[None, :],
                    writes=[twstage], dsem="DW0")
                for j in range(2):
                    bank, tb = PS[1 + j]
                    b.op("T", "matmul", bank[:], onesrow[:],
                         wstage[0:1, j * 512:(j + 1) * 512], start=True, stop=True,
                         reads=[tonesr, twstage], writes=[tb])
                    b.op("V", "tensor_copy", b2bc[:, j * 512:(j + 1) * 512],
                         bank[:], reads=[tb], writes=[tb2bc])
                schain = [Buf(f"schain{i}") for i in range(8)]
                _sc = [0]
                def sdma(dst, srcap, rd, wr):
                    c = _sc[0] % 8; _sc[0] += 1
                    dma(dst, srcap, reads=rd, writes=wr + [schain[c]],
                        dsem=f"DS{c}")
                for piece in range(4):
                    for kk in range(8):
                        sdma(W1_bf[:, kk * FFN + piece * 1024:
                                   kk * FFN + (piece + 1) * 1024],
                             wbf_dram["w1"].ap()[kk * 128:(kk + 1) * 128,
                                                 piece * 1024:(piece + 1) * 1024],
                             [twbfd["w1"]], [tW1[kk][piece]])
                for kk in range(32):
                    sdma(W2_bf[:, kk * D:(kk + 1) * D],
                         wbf_dram["w2"].ap()[kk * 128:(kk + 1) * 128, :],
                         [twbfd["w2"]], [tW2[kk]])
                def emit_preB_ln(mj):
                    if mj >= n_macros:
                        return
                    s2 = mj % SEQ_PER_CORE
                    t02 = (mj // SEQ_PER_CORE) * MACRO
                    x2b, tx2b = xPB[mj % 2]
                    yT2, tyT2 = yTPB[mj % 2]
                    for sub in range(NSUB):
                        dma(x2b[:, sub * D:(sub + 1) * D],
                            x2_dram.ap()[s2, t02 + sub * 128:
                                         t02 + (sub + 1) * 128, :],
                            reads=[tx2d], writes=[tx2b],
                            dsem=("DXB0" if mj % 2 == 0 else "DXB1"))
                    for sub in range(NSUB):
                        layernorm(x2b[:, sub * D:(sub + 1) * D],
                                  yT2[:, sub * D:(sub + 1) * D], tyT2, tx2b,
                                  yT2[:, sub * D:(sub + 1) * D], tyT2)
                def emit_preB_tr(mj):
                    if mj >= n_macros:
                        return
                    lnT2, tlnT2 = lnTPB[mj % 2]
                    yT2, tyT2 = yTPB[mj % 2]
                    for grp in range(2):
                        pbank, tpbank = PS[0]
                        idx = [(grp * 4 + i % 4, i // 4) for i in range(8)]
                        pe_transpose_8(
                            lambda i: yT2[:, idx[i][1] * D + idx[i][0] * 128:
                                          idx[i][1] * D + (idx[i][0] + 1) * 128],
                            None, tyT2, tlnT2, pbank, tpbank, evac_eng="S",
                            dst_batched=(_perm_out(lnT2, grp), _perm_in(pbank)))
                emit_preB_ln(0)
                emit_preB_tr(0)
                for mi in range(n_macros):
                    s = mi % SEQ_PER_CORE
                    mb = mi // SEQ_PER_CORE
                    t0 = mb * MACRO
                    x_sb, tx = xPB[mi % 2]
                    lnT, tlnT = lnTPB[mi % 2]
                    yT_bf, tyT = yTPB[mi % 2]
                    emit_preB_ln(mi + 1)
                    for hs in range(32):
                        bank, tb = PS[hs % 4]
                        for kk in range(8):
                            b.op("T", "matmul", bank[:, 0:MACRO],
                                 W1_bf[:, kk * FFN + hs * 128:
                                       kk * FFN + (hs + 1) * 128],
                                 lnT[:, kk * MACRO:(kk + 1) * MACRO],
                                 start=(kk == 0), stop=(kk == 7),
                                 reads=[tW1[kk][hs // 8], tlnT], writes=[tb], inc=(kk == 7))
                        b.op("S", "activation", hfm[:, hs * MACRO:(hs + 1) * MACRO],
                             bank[:, 0:MACRO], AF.Relu, bias=b1c[:, hs:hs + 1],
                             reads=[tb, tb1c], writes=[thfm])
                    for sub in range(NSUB):
                        for j in range(2):
                            bank, tb = PS[4 + sub * 2 + j]
                            for hs in range(32):
                                b.op("T", "matmul", bank[:],
                                     hfm[:, hs * MACRO + sub * 128:
                                         hs * MACRO + (sub + 1) * 128],
                                     W2_bf[:, hs * D + j * 512: hs * D + (j + 1) * 512],
                                     start=(hs == 0), stop=(hs == 31),
                                     reads=[thfm, tW2[hs]], writes=[tb], inc=(hs == 31))
                            b.op("V", "tensor_tensor",
                                 y_sb[:, sub * D + j * 512: sub * D + (j + 1) * 512],
                                 bank[:], b2bc[:, j * 512:(j + 1) * 512], ALU.add,
                                 reads=[tb, tb2bc], writes=[ty])
                            b.op("P", "tensor_tensor",
                                 y_sb[:, sub * D + j * 512: sub * D + (j + 1) * 512],
                                 y_sb[:, sub * D + j * 512: sub * D + (j + 1) * 512],
                                 x_sb[:, sub * D + j * 512: sub * D + (j + 1) * 512],
                                 ALU.add, reads=[ty, tx], writes=[ty])
                        dma(out_h.ap()[s, t0 + sub * 128: t0 + (sub + 1) * 128, :],
                            y_sb[:, sub * D:(sub + 1) * D], reads=[ty],
                            writes=[toutc], dsem="DO")
                    emit_preB_tr(mi + 1)
                b.wait_everything("Y")
                b.flush(blk)

    return nc


def _in_maps(x, weights):
    maps = []
    for c in range(NCORES):
        m = {"x": np.ascontiguousarray(x[c * SEQ_PER_CORE:(c + 1) * SEQ_PER_CORE])}
        m.update(weights)
        maps.append(m)
    return maps


def kernel(L0_attn_x, L0_attn_Wq, L0_attn_Wk, L0_attn_Wv, L0_attn_Wf1,
           L0_attn_Wf2, L0_attn_Wbeta, L0_attn_Wog1, L0_attn_Wog2,
           L0_attn_norm_w, L0_attn_Wo, L0_ffn_w1, L0_ffn_b1, L0_ffn_w2,
           L0_ffn_b2):
    x = np.ascontiguousarray(np.asarray(L0_attn_x, dtype=np.float32))
    weights = {
        "Wq": L0_attn_Wq, "Wk": L0_attn_Wk, "Wv": L0_attn_Wv,
        "Wf1": L0_attn_Wf1, "Wf2": L0_attn_Wf2, "Wbeta": L0_attn_Wbeta,
        "Wog1": L0_attn_Wog1, "Wog2": L0_attn_Wog2, "norm_w": L0_attn_norm_w,
        "Wo": L0_attn_Wo, "w1": L0_ffn_w1, "b1": L0_ffn_b1,
        "w2": L0_ffn_w2, "b2": L0_ffn_b2,
    }
    weights = {k: np.ascontiguousarray(np.asarray(v, dtype=np.float32))
               for k, v in weights.items()}
    nc = build_program()
    res = run_bass_kernel_spmd(nc, _in_maps(x, weights),
                               core_ids=list(range(NCORES)))
    return np.concatenate([r["out"] for r in res.results], axis=0)


# version 31
# speedup vs baseline: 1.0338x; 1.0338x over previous
import numpy as np
import ml_dtypes
from contextlib import ExitStack

import concourse.bass as bass
import concourse.mybir as mybir
from concourse.bass_utils import run_bass_kernel_spmd

F32 = mybir.dt.float32
BF16 = mybir.dt.bfloat16
AF = mybir.ActivationFunctionType
ALU = mybir.AluOpType
BF = ml_dtypes.bfloat16
AX = mybir.AxisListType.X

BATCH, NCORES, SEQ, D, H, HD = 16, 8, 2048, 1024, 8, 128
C, L, MACRO, FFN = 64, 128, 256, 4096
NSUB, NCH = MACRO // 128, MACRO // C
SEQ_PER_CORE = BATCH // NCORES
MACROS_PER_SEQ = SEQ // MACRO
EPS_LN, EPS_K, EPS_RMS = 1e-5, 1e-6, 1e-6

ENGS = (("T", "V", "S", "P", "Y", "DC", "DW0", "DW1", "DX", "DSP", "DO", "DT2", "DP0", "DP1", "DB") + tuple(f"DS{i}" for i in range(8)) + tuple(f"DU{i}" for i in range(8)) + ("DX1", "DXB0", "DXB1") + tuple(f"DK{i}" for i in range(8)))


class _Sem:
    def __init__(self, nc, name):
        self.name, self.cm, self.h, self.n = name, nc.semaphore(name), None, 0


class Buf:
    __slots__ = ("name", "last_w", "readers")

    def __init__(self, name):
        self.name, self.last_w, self.readers = name, None, []


class Bld:

    def __init__(self, nc):
        self.nc = nc
        self.sems = {e: _Sem(nc, f"sem_{e}") for e in ENGS}
        self.q = {e: [] for e in ENGS}
        self.waited = {}
        self.pending = {e: [] for e in ENGS}

    def enter(self, ctx):
        for s in self.sems.values():
            s.h = ctx.enter_context(s.cm)

    def _emit_wait(self, eng, tok):
        if tok is None:
            return
        peng, sem, val = tok
        if val is None:
            assert peng == eng, f"cross-engine read of pending token ({sem.name})"
            return
        if peng == eng and eng == "T":
            return
        key = (eng, sem.name)
        if self.waited.get(key, 0) >= val:
            return
        self.waited[key] = val
        self.q[eng].append(lambda e, s=sem, v=val: e.wait_ge(s.h, v))

    def op(self, eng, fname, *args, reads=(), writes=(), inc=True, dma=False,
           dsem=None, **kw):
        for bb in reads:
            self._emit_wait(eng, bb.last_w)
        for bb in writes:
            self._emit_wait(eng, bb.last_w)
            for t in bb.readers:
                self._emit_wait(eng, t)
        semname = dsem if dsem is not None else eng
        sem = self.sems[semname]
        amt = 16 if dma else 1
        if inc:
            sem.n += amt
            tok = (semname, sem, sem.n)
        else:
            tok = None

        def emit(e, fname=fname, args=args, kw=kw, tok=tok, amt=amt):
            inst = getattr(e, fname)(*args, **kw)
            if tok is not None:
                inst.then_inc(tok[1].h, amt)
            return inst

        self.q[eng].append(emit)
        if tok is not None:
            for bb in self.pending[eng]:
                if bb.last_w is not None and bb.last_w[2] is None:
                    bb.last_w = tok
                else:
                    bb.readers.append(tok)
            self.pending[eng] = []
            for bb in reads:
                bb.readers.append(tok)
            for bb in writes:
                bb.last_w, bb.readers = tok, []
        else:
            for bb in reads:
                self.pending[eng].append(bb)
            for bb in writes:
                bb.last_w, bb.readers = (eng, sem, None), []
                self.pending[eng].append(bb)
        return tok

    def settle(self, semname, bufs):
        s = self.sems[semname]
        for bb in bufs:
            bb.last_w = (semname, s, s.n)

    def wait_self(self, eng):
        s = self.sems[eng]
        if s.n:
            key = (eng, s.name)
            if self.waited.get(key, 0) < s.n:
                self.waited[key] = s.n
                self.q[eng].append(lambda e, ss=s, v=s.n: e.wait_ge(ss.h, v))

    def wait_everything(self, eng):
        for e2 in ENGS:
            s = self.sems[e2]
            if s.n:
                if e2 == eng:
                    self.wait_self(eng)
                else:
                    self._emit_wait(eng, (e2, s, s.n))

    def flush(self, blk):
        sect = {"T": blk.tensor, "V": blk.vector, "S": blk.scalar,
                "P": blk.gpsimd, "Y": blk.sync}
        for e in ("T", "V", "S", "P", "Y"):
            ops = self.q[e]
            if ops:
                def body(engine, ops=tuple(ops)):
                    for cfn in ops:
                        cfn(engine)
                sect[e](body)
                self.q[e] = []


def _masks():
    t = np.arange(1, C + 1)
    lt = (t[:, None] > t[None, :])
    le = (t[:, None] >= t[None, :])
    mask_T = np.zeros((L, L), np.float32)
    mask_T[:C, :C] = lt; mask_T[:C, C:] = lt
    mask_T[C:, :C] = le; mask_T[C:, C:] = lt
    mask_W = np.concatenate([lt.T, lt.T], axis=1)
    mask_HQ = np.concatenate([le.T, le.T], axis=0)
    mask_KQ = le.T
    return mask_T, mask_T.T, mask_W, mask_HQ, mask_KQ


def build_program(n_macros=None, taps=()):
    nc = bass.Bass("TRN2", target_bir_lowering=False)
    total_macros = SEQ_PER_CORE * MACROS_PER_SEQ
    if n_macros is None:
        n_macros = total_macros

    x_h = nc.dram_tensor("x", [SEQ_PER_CORE, SEQ, D], F32, kind="ExternalInput")
    wh = {}
    for nm, shp in [("Wq", [D, D]), ("Wk", [D, D]), ("Wv", [D, D]),
                    ("Wf1", [D, HD]), ("Wf2", [HD, D]), ("Wbeta", [D, H]),
                    ("Wog1", [D, HD]), ("Wog2", [HD, D]), ("norm_w", [D]),
                    ("Wo", [D, D]), ("w1", [D, FFN]), ("b1", [FFN]),
                    ("w2", [FFN, D]), ("b2", [D])]:
        wh[nm] = nc.dram_tensor(nm, shp, F32, kind="ExternalInput")
    out_h = nc.dram_tensor("out", [SEQ_PER_CORE, SEQ, D], F32, kind="ExternalOutput")
    x2_dram = nc.dram_tensor("x2scratch", [SEQ_PER_CORE, SEQ, D], F32,
                             kind="Internal")
    wbf_dram = {nm: nc.dram_tensor(f"wbf_{nm}", [128, 8 * D], BF16, kind="Internal")
                for nm in ("Wq", "Wk")}
    wbf_dram["Wv"] = nc.dram_tensor("wbf_Wv", [D, D], BF16, kind="Internal")
    wbf_dram["Wo"] = nc.dram_tensor("wbf_Wo", [D, D], BF16, kind="Internal")
    wbf_dram["w1"] = nc.dram_tensor("wbf_w1", [D, FFN], BF16, kind="Internal")
    wbf_dram["w2"] = nc.dram_tensor("wbf_w2", [FFN, D], BF16, kind="Internal")
    twbfd = {nm: Buf(f"wbf_{nm}") for nm in ("Wq", "Wk", "Wv", "Wo", "w1", "w2")}
    tx2d = Buf("x2_dram")
    tap_h = {}
    _tapspec = {"ln1": ([128, NSUB * D], BF16), "lnT": ([128, 8 * MACRO], BF16),
                "q": ([128, H * MACRO], BF16), "k": ([128, H * MACRO], F32),
                "khat": ([128, H * MACRO], BF16), "lam": ([128, H * MACRO], F32),
                "Lam": ([128, H * MACRO], F32), "sig": ([128, H * MACRO], BF16),
                "v": ([64, NCH * D], BF16), "gate": ([128, NSUB * D], BF16),
                "y": ([128, NSUB * D], F32), "x2": ([128, NSUB * D], F32),
                "S0": ([128, 16 * HD], F32), "U": ([128, 1024], BF16),
                "Tm": ([128, 512], BF16), "Bm": ([128, 1024], BF16),
                "Am": ([128, 1024], BF16)}
    for tname in taps:
        shp, dt_ = _tapspec[tname]
        tap_h[tname] = nc.dram_tensor("tap_" + tname, shp, dt_,
                                      kind="ExternalOutput")

    mT, mTt, mW, mHQ, mKQ = _masks()
    c_maskT = nc.inline_tensor(np.tile(mT, (1, 4)).astype(BF), name="c_maskT")
    c_maskTt = nc.inline_tensor(np.tile(mTt, (1, 4)).astype(BF), name="c_maskTt")
    c_maskW = nc.inline_tensor(np.tile(mW, (1, 4)).astype(BF), name="c_maskW")
    c_maskHQ = nc.inline_tensor(np.tile(mHQ, (1, 8)).astype(BF), name="c_maskHQ")
    c_maskKQ = nc.inline_tensor(np.tile(mKQ, (1, 8)).astype(BF), name="c_maskKQ")
    c_ident = nc.inline_tensor(np.eye(128, dtype=BF), name="c_ident")
    c_ident32 = nc.inline_tensor(np.eye(128, dtype=np.float32), name="c_ident32")
    c_ones_col = nc.inline_tensor(np.ones((128, 1), dtype=BF), name="c_ones_col")
    c_ones_row = nc.inline_tensor(np.ones((1, 128), dtype=np.float32),
                                  name="c_ones_row")
    bc8 = np.zeros((8, 8 * 128), dtype=BF)
    for hh in range(8):
        bc8[hh, hh * 128:(hh + 1) * 128] = -2.0
    c_bcast8 = nc.inline_tensor(bc8, name="c_bcast8")

    b = Bld(nc)

    with ExitStack() as ctx:
        b.enter(ctx)

        def mk(ctx_, name, shape, dt=F32):
            import os as _os
            if _os.environ.get("KDBG_SBUF"):
                print(f"  alloc {name} {shape} {dt}: remaining {nc.sbuf_bytes_remaining}")
            t = ctx_.enter_context(nc.sbuf_tensor(name, shape, dt))
            return t, Buf(name)

        wstage, twstage = mk(ctx, "wstage", [128, 1024], F32)
        consts, tconsts = mk(ctx, "consts", [128, 1664], BF16)
        consts2, tconsts2 = mk(ctx, "consts2", [64, 768], BF16)
        ident32, tid32 = mk(ctx, "ident32", [128, 128], F32)
        onescol, tones = mk(ctx, "onescol", [128, 1], BF16)
        onesrow, tonesr = mk(ctx, "onesrow", [1, 128], F32)
        bcast8, tbc8 = mk(ctx, "bcast8", [8, 1024], BF16)
        xP = [mk(ctx, "x_sb0", [128, NSUB * D], F32)]
        lnTP = [mk(ctx, "lnT0", [128, 8 * MACRO], BF16)]
        yTP = [mk(ctx, "yT_bf0", [128, NSUB * D], BF16)]

        y_sb, ty = mk(ctx, "y_sb", [128, NSUB * D], F32)
        lncol, tlncol = mk(ctx, "lncol", [128, 16], F32)
        epsc, tepsc = mk(ctx, "epsc", [128, 4], F32)
        PS = []
        for i in range(8):
            t = ctx.enter_context(nc.psum_tensor(f"PB{i}", [128, 512], F32))
            PS.append((t, Buf(f"PB{i}")))

        def dma(dst_ap, src_ap, reads=(), writes=(), q="Y", dsem="DX"):
            return b.op(q, "dma_start", dst_ap, src_ap, reads=reads,
                        writes=writes, dma=True, dsem=dsem)

        IDB = consts[:, 1536:1664]

        def pe_transpose_8(src_fn, dst_fn, tsrc, tdst, pbank, tpbank, n=8,
                           rows=128, dst_batched=None, evac_eng="V"):
            for i in range(n):
                b.op("T", "transpose",
                     pbank[:].bitcast(BF16)[0:rows, i * 128:(i + 1) * 128]
                     if rows != 128 else
                     pbank[:].bitcast(BF16)[:, i * 128:(i + 1) * 128],
                     src_fn(i), IDB,
                     reads=[tsrc, tconsts], writes=[tpbank], inc=(i == n - 1))
            if dst_batched is not None:
                out_ap, in_ap = dst_batched
                b.op(evac_eng, "tensor_copy" if evac_eng == "V" else "copy",
                     out_ap, in_ap, reads=[tpbank], writes=[tdst])
                return
            for i in range(n):
                b.op("V", "tensor_copy", dst_fn(i),
                     pbank[:].bitcast(BF16)[0:rows, i * 128:(i + 1) * 128]
                     if rows != 128 else
                     pbank[:].bitcast(BF16)[:, i * 128:(i + 1) * 128],
                     reads=[tpbank], writes=[tdst], inc=(i == n - 1))

        def layernorm(xs, dst, tdst_, tx_, sq_tile, tsq_):
            b.op("V", "reduce_sum", lncol[:, 0:1], xs, axis=AX,
                 reads=[tx_], writes=[tlncol])
            b.op("V", "tensor_scalar_mul", lncol[:, 1:2], lncol[:, 0:1], 1.0 / D,
                 reads=[tlncol], writes=[tlncol])
            b.op("S", "activation", sq_tile, xs, AF.Square,
                 accum_out=lncol[:, 2:3], reads=[tx_], writes=[tsq_, tlncol])
            b.op("V", "tensor_scalar_mul", lncol[:, 3:4], lncol[:, 2:3], 1.0 / D,
                 reads=[tlncol], writes=[tlncol])
            b.op("V", "tensor_tensor", lncol[:, 4:5], lncol[:, 1:2], lncol[:, 1:2],
                 ALU.mult, reads=[tlncol], writes=[tlncol])
            b.op("V", "tensor_tensor", lncol[:, 5:6], lncol[:, 3:4], lncol[:, 4:5],
                 ALU.subtract, reads=[tlncol], writes=[tlncol])
            b.op("S", "activation", lncol[:, 6:7], lncol[:, 5:6], AF.Ln,
                 bias=epsc[:, 0:1], reads=[tlncol, tepsc], writes=[tlncol])
            b.op("S", "activation", lncol[:, 7:8], lncol[:, 6:7], AF.Exp,
                 scale=-0.5, reads=[tlncol], writes=[tlncol])
            b.op("V", "tensor_scalar", dst, xs, lncol[:, 1:2], lncol[:, 7:8],
                 ALU.subtract, ALU.mult, reads=[tx_, tlncol], writes=[tdst_])

        def _perm_in(pbank):
            return pbank[:].bitcast(BF16).rearrange(
                "p (s k e) -> p k s e", s=2, k=4)

        def _perm_out(dst, grp):
            return dst[:, grp * 1024:(grp + 1) * 1024].rearrange(
                "p (k s e) -> p k s e", k=4, s=2)

        def ln_and_transpose(x_sb, tx, lnT, tlnT, yT_bf, tyT):
            for sub in range(NSUB):
                layernorm(x_sb[:, sub * D:(sub + 1) * D],
                          yT_bf[:, sub * D:(sub + 1) * D], tyT, tx,
                          yT_bf[:, sub * D:(sub + 1) * D], tyT)
            for grp in range(2):
                pbank, tpbank = PS[7]
                idx = [(grp * 4 + i % 4, i // 4) for i in range(8)]
                pe_transpose_8(
                    lambda i: yT_bf[:, idx[i][1] * D + idx[i][0] * 128:
                                    idx[i][1] * D + (idx[i][0] + 1) * 128],
                    None, tyT, tlnT, pbank, tpbank,
                    dst_batched=(_perm_out(lnT, grp), _perm_in(pbank)))

        tapchain = Buf("tapchain")
        toutc = Buf("outchain")
        def tap(name, src_ap, reads):
            if name in tap_h:
                b.op("Y", "dma_start", tap_h[name].ap(), src_ap,
                     reads=list(reads), writes=[tapchain], dma=True, dsem="DT2")

        with ExitStack() as ctxA:
            scrA, tscrA = mk(ctxA, "scrA", [128, 2048], F32)
            xP.append(mk(ctxA, "x_sb1", [128, NSUB * D], F32))
            lnTP.append(mk(ctxA, "lnT1", [128, 8 * MACRO], BF16))
            yTP.append(mk(ctxA, "yT_bf1", [128, NSUB * D], BF16))
            Wbuf = [(ctxA.enter_context(nc.sbuf_tensor("Wbuf0", [128, 8 * D], BF16)),
                     [Buf(f"Wbuf0.{i}") for i in range(8)]),
                    (ctxA.enter_context(nc.sbuf_tensor("Wbuf1", [128, 8 * D], BF16)),
                     [Buf(f"Wbuf1.{i}") for i in range(8)])]
            Wf1_bf, tWf1 = mk(ctxA, "Wf1_bf", [128, 8 * HD], BF16)
            Wf2_bf, tWf2 = mk(ctxA, "Wf2_bf", [128, D], BF16)
            Wog1_bf, tWog1 = mk(ctxA, "Wog1_bf", [128, 8 * HD], BF16)
            Wog2_bf, tWog2 = mk(ctxA, "Wog2_bf", [128, D], BF16)
            Wbe_bf, tWbe = mk(ctxA, "Wbe_bf", [128, 8 * H], BF16)
            nw_col, tnw = mk(ctxA, "nw_col", [128, 8], F32)
            S0_f32, tS0 = mk(ctxA, "S0_f32", [128, 16 * HD], F32)
            S0_bf, tS0b = mk(ctxA, "S0_bf", [128, 16 * HD], BF16)
            q_bf, tq = mk(ctxA, "q_bf", [128, H * MACRO], BF16)
            k_f32, tk = mk(ctxA, "k_f32", [128, H * MACRO], F32)
            khat, tkh = mk(ctxA, "khat", [128, H * MACRO], BF16)
            lam, tlam = mk(ctxA, "lam", [128, H * MACRO], F32)
            Lam, tLam = mk(ctxA, "Lam", [128, H * MACRO], F32)
            tLamC = [tLam, Buf("LamC1"), Buf("LamC2"), Buf("LamC3")]
            sigbc, tsig = mk(ctxA, "sigbc", [128, H * MACRO], BF16)
            gate_bf, tgate = mk(ctxA, "gate_bf", [128, NSUB * D], BF16)
            v_bf, tv = mk(ctxA, "v_bf", [64, NCH * D], BF16)
            fh_bf, tfh = mk(ctxA, "fh_bf", [128, MACRO], BF16)
            ogh_bf, togh = mk(ctxA, "ogh_bf", [128, MACRO], BF16)
            bsig, tbsig = mk(ctxA, "bsig", [8, MACRO], BF16)
            sqrow, tsqrow = mk(ctxA, "sqrow", [1, 512], F32)
            sqbf, tsqbf = mk(ctxA, "sqbf", [1, 512], BF16)
            sqrowB, tsqrow2 = mk(ctxA, "sqrowB", [1, 512], F32)
            sqbfB, tsqbf2 = mk(ctxA, "sqbfB", [1, 512], BF16)
            rmscol, trms = mk(ctxA, "rmscol", [128, 16], F32)
            eX, teX = mk(ctxA, "eX", [128, 512], F32)
            teXB = Buf("eXB")
            lexcP = [mk(ctxA, "lexc0", [128, 512], F32),
                     mk(ctxA, "lexc1", [128, 512], F32)]
            t1s, tt1 = mk(ctxA, "t1s", [128, 512], F32)
            BmP = [mk(ctxA, "Bm0", [128, 1024], BF16),
                   mk(ctxA, "Bm1", [128, 1024], BF16)]
            AmP = [mk(ctxA, "Am0", [128, 1024], BF16),
                   mk(ctxA, "Am1", [128, 1024], BF16)]
            KtP = [mk(ctxA, "Kt0", [128, 512], BF16),
                   mk(ctxA, "Kt1", [128, 512], BF16)]
            qtP = [mk(ctxA, "qt0", [128, 512], BF16),
                   mk(ctxA, "qt1", [128, 512], BF16)]
            AtmP = [mk(ctxA, "Atm0", [128, 1024], BF16),
                    mk(ctxA, "Atm1", [128, 1024], BF16)]
            KtmP = [mk(ctxA, "Ktm0", [64, 1024], BF16),
                    mk(ctxA, "Ktm1", [64, 1024], BF16)]
            T_sb = [mk(ctxA, "T_sb0", [128, 512], BF16),
                    mk(ctxA, "T_sb1", [128, 512], BF16)]
            Pm_sb = [mk(ctxA, "Pm_sb0", [128, 512], BF16),
                     mk(ctxA, "Pm_sb1", [128, 512], BF16)]
            X_sb = [mk(ctxA, "X_sb0", [128, 512], BF16),
                    mk(ctxA, "X_sb1", [128, 512], BF16)]
            W_sb, tWsb = mk(ctxA, "W_sb", [64, 1024], BF16)
            HQ_sb, tHQ = mk(ctxA, "HQ_sb", [128, 512], BF16)
            KQ_sb, tKQ = mk(ctxA, "KQ_sb", [64, 512], BF16)
            U_sb, tU = mk(ctxA, "U_sb", [128, 1024], BF16)
            tUg = [tU, Buf("U_sb_g1")]
            egcP = [mk(ctxA, "egc0", [128, 8], F32),
                    mk(ctxA, "egc1", [128, 8], F32)]
            Ssc, tSsc = scrA[:, 1024:1536], tscrA

            blk = nc.Block()
            blk.__enter__()
            if True:
                dma(consts[:, 0:512], c_maskT.ap(), writes=[tconsts], dsem="DC")
                dma(consts[:, 512:1024], c_maskTt.ap(), writes=[tconsts], dsem="DC")
                dma(consts[:, 1024:1536], c_maskHQ.ap(), writes=[tconsts], dsem="DC")
                dma(consts[:, 1536:1664], c_ident.ap(), writes=[tconsts], dsem="DC")
                dma(consts2[:, 0:512], c_maskW.ap(), writes=[tconsts2], dsem="DC")
                dma(consts2[:, 512:768], c_maskKQ.ap()[:, 0:256], writes=[tconsts2], dsem="DC")
                dma(onescol[:], c_ones_col.ap(), writes=[tones], dsem="DC")
                dma(onesrow[:], c_ones_row.ap(), writes=[tonesr], dsem="DC")
                dma(bcast8[:], c_bcast8.ap(), writes=[tbc8], dsem="DC")
                dma(ident32[:], c_ident32.ap(), writes=[tid32], dsem="DC")
                dma(wstage[0:1, 0:1024], wh["norm_w"].ap()[None, :],
                    writes=[twstage], dsem="DW0")
                b.settle("DC", [tconsts, tconsts2, tones, tonesr, tbc8, tid32])
                b.op("V", "memset", epsc[:, 0:1], EPS_LN, writes=[tepsc])
                b.op("V", "memset", epsc[:, 1:2], EPS_K, writes=[tepsc])
                b.op("V", "memset", epsc[:, 2:3], EPS_RMS, writes=[tepsc])
                b.op("V", "memset", epsc[:, 3:4], 1.0, writes=[tepsc])
                b.op("V", "memset", S0_f32[:], 0.0, writes=[tS0])
                b.op("V", "memset", S0_bf[:], 0.0, writes=[tS0b])
                pb, tpb = PS[0]
                for kk in range(8):
                    b.op("T", "transpose", pb[:, kk:kk + 1],
                         wstage[0:1, kk * 128:(kk + 1) * 128], ident32[0:1, 0:1],
                         reads=[twstage, tid32], writes=[tpb], inc=(kk == 7))
                b.op("S", "copy", nw_col[:], pb[:, 0:8], reads=[tpb], writes=[tnw])

                _wl_par = [0]
                def load_weight(dst, tdst, src_h, ktiles, width, scale=False):
                    tl = tdst if isinstance(tdst, list) else [tdst] * ktiles
                    for kk in range(ktiles):
                        par = _wl_par[0] % 2
                        _wl_par[0] += 1
                        stg = (wstage[:, 0:width] if par == 0
                               else scrA[:, 0:width].bitcast(F32) if False
                               else scrA[:, 0:width])
                        tstg = twstage if par == 0 else tscrA
                        dma(stg, src_h.ap()[kk * 128:(kk + 1) * 128, :],
                            writes=[tstg], dsem=("DW0" if par == 0 else "DW1"))
                        if scale:
                            b.op("V", "tensor_scalar_mul",
                                 dst[:, kk * width:(kk + 1) * width], stg,
                                 nw_col[:, kk:kk + 1],
                                 reads=[tstg, tnw], writes=[tl[kk]])
                        else:
                            b.op("V", "tensor_copy",
                                 dst[:, kk * width:(kk + 1) * width], stg,
                                 reads=[tstg], writes=[tl[kk]])
                _wload_done = [False]
                def _do_weight_loads():
                    if _wload_done[0]:
                        return
                    _wload_done[0] = True
                    _weight_loads()
                def _weight_loads():
                    tbgchain = Buf("bgchain")
                    b.op("P", "dma_start", wbf_dram["Wv"].ap(), wh["Wv"].ap(),
                         writes=[twbfd["Wv"], tbgchain], dma=True, dsem="DB")
                    b.op("P", "dma_start", wbf_dram["Wo"].ap(), wh["Wo"].ap(),
                         writes=[twbfd["Wo"], tbgchain], dma=True, dsem="DB")
                    b.op("P", "dma_start", wbf_dram["w1"].ap(), wh["w1"].ap(),
                         writes=[twbfd["w1"], tbgchain], dma=True, dsem="DB")
                    b.op("P", "dma_start", wbf_dram["w2"].ap(), wh["w2"].ap(),
                         writes=[twbfd["w2"], tbgchain], dma=True, dsem="DB")
                    for nm, (wcast, twcast) in (("Wq", Wbuf[0]), ("Wk", Wbuf[1])):
                        for kk in range(8):
                            par = kk % 2
                            stg = wstage[:, 0:D] if par == 0 else scrA[:, 0:D]
                            tstg = twstage if par == 0 else tscrA
                            dma(stg, wh[nm].ap()[kk * 128:(kk + 1) * 128, :],
                                writes=[tstg], dsem=("DW0" if par == 0 else "DW1"))
                            b.op("V", "tensor_copy",
                                 wcast[:, kk * D:(kk + 1) * D], stg,
                                 reads=[tstg], writes=[twcast[kk]])
                        dma(wbf_dram[nm].ap(), wcast[:], reads=list(twcast),
                            writes=[twbfd[nm], tx2d], dsem="DSP")
                    load_weight(Wf1_bf, tWf1, wh["Wf1"], 8, HD)
                    load_weight(Wf2_bf, tWf2, wh["Wf2"], 1, D)
                    load_weight(Wog1_bf, tWog1, wh["Wog1"], 8, HD)
                    load_weight(Wog2_bf, tWog2, wh["Wog2"], 1, D)
                    load_weight(Wbe_bf, tWbe, wh["Wbeta"], 8, H)

            def chunk3(tn, cc):
                r = tn[:].rearrange("p (h t) -> p h t", h=H)
                return r[:, :, cc * C:(cc + 1) * C]

            def bl3(tn):
                ap = tn[:] if hasattr(tn, "shape") and not isinstance(tn, bass.AP) else tn
                try:
                    ap = tn[:]
                except TypeError:
                    ap = tn
                return ap.rearrange("p (h t) -> p h t", h=H)

            if True:
                def emit_preA(mj):
                    if mj >= n_macros:
                        return
                    s2 = mj % SEQ_PER_CORE
                    t02 = (mj // SEQ_PER_CORE) * MACRO
                    x2b, tx2b = xP[mj % 2]
                    lnT2, tlnT2 = lnTP[mj % 2]
                    yT2, tyT2 = yTP[mj % 2]
                    for sub in range(NSUB):
                        dma(x2b[:, sub * D:(sub + 1) * D],
                            x_h.ap()[s2, t02 + sub * 128: t02 + (sub + 1) * 128, :],
                            writes=[tx2b], dsem=("DX" if mj % 2 == 0 else "DX1"))
                    ln_and_transpose(x2b, tx2b, lnT2, tlnT2, yT2, tyT2)
                def emit_qproj(mj):
                    if mj >= n_macros:
                        return
                    lnT2, tlnT2 = lnTP[mj % 2]
                    for hh in range(H):
                        bank, tb = PS[hh % 4]
                        for kk in range(8):
                            b.op("T", "matmul", bank[:, 0:MACRO],
                                 Wbuf[0][0][:, kk * D + hh * HD:
                                             kk * D + (hh + 1) * HD],
                                 lnT2[:, kk * MACRO:(kk + 1) * MACRO],
                                 start=(kk == 0), stop=(kk == 7),
                                 reads=[Wbuf[0][1][kk], tlnT2], writes=[tb],
                                 inc=(kk == 7))
                        b.op("S", "activation",
                             q_bf[:, hh * MACRO:(hh + 1) * MACRO],
                             bank[:, 0:MACRO], AF.Silu,
                             reads=[tb], writes=[tq])
                    for kk in range(8):
                        dma(Wbuf[0][0][:, kk * D:(kk + 1) * D],
                            wbf_dram["Wv"].ap()[kk * 128:(kk + 1) * 128, :],
                            reads=[twbfd["Wv"]], writes=[Wbuf[0][1][kk]],
                            dsem=f"DS{kk}")
                emit_preA(0)
                _do_weight_loads()
                emit_qproj(0)
                for mi in range(n_macros):
                    s = mi % SEQ_PER_CORE
                    mb = mi // SEQ_PER_CORE
                    t0 = mb * MACRO
                    last_macro = mi == n_macros - 1
                    x_sb, tx = xP[mi % 2]
                    lnT, tlnT = lnTP[mi % 2]
                    yT_bf, tyT = yTP[mi % 2]
                    if last_macro:
                        tap("ln1", yT_bf[:], [tyT])
                        tap("lnT", lnT[:], [tlnT])
                    for dst, tdst, (Wt, tWt) in ((k_f32, tk, Wbuf[1]),):
                        for hh in range(H):
                            bank, tb = PS[hh % 4]
                            for kk in range(8):
                                b.op("T", "matmul", bank[:, 0:MACRO],
                                     Wt[:, kk * D + hh * HD: kk * D + (hh + 1) * HD],
                                     lnT[:, kk * MACRO:(kk + 1) * MACRO],
                                     start=(kk == 0), stop=(kk == 7),
                                     reads=[tWt[kk], tlnT], writes=[tb], inc=(kk == 7))
                            b.op("S", "activation",
                                 dst[:, hh * MACRO:(hh + 1) * MACRO],
                                 bank[:, 0:MACRO], AF.Silu,
                                 reads=[tb], writes=[tdst])
                    for kk in range(8):
                        dma(Wbuf[1][0][:, kk * D:(kk + 1) * D],
                            wbf_dram["Wo"].ap()[kk * 128:(kk + 1) * 128, :],
                            reads=[twbfd["Wo"]], writes=[Wbuf[1][1][kk]],
                            dsem=f"DU{kk}")
                    for sub in range(NSUB):
                        for j in range(2):
                            bank, tb = PS[(sub * 2 + j) % 4]
                            for kk in range(8):
                                b.op("T", "matmul", bank[:],
                                     lnT[:, kk * MACRO + sub * 128:
                                         kk * MACRO + (sub + 1) * 128],
                                     Wbuf[0][0][:, kk * D + j * 512:
                                                 kk * D + (j + 1) * 512],
                                     start=(kk == 0), stop=(kk == 7),
                                     reads=[tlnT, Wbuf[0][1][kk]], writes=[tb],
                                     inc=(kk == 7))
                            for half in range(2):
                                ch = 2 * sub + half
                                b.op("S", "activation",
                                     v_bf[:, ch * D + j * 512: ch * D + (j + 1) * 512],
                                     bank[half * 64:(half + 1) * 64, :], AF.Silu,
                                     reads=[tb], writes=[tv])
                    for dst, tdst, Wt, tWt, bankno in ((fh_bf, tfh, Wf1_bf, tWf1, 4),
                                                       (ogh_bf, togh, Wog1_bf, tWog1, 5)):
                        bank, tb = PS[bankno]
                        for kk in range(8):
                            b.op("T", "matmul", bank[:, 0:MACRO],
                                 Wt[:, kk * HD:(kk + 1) * HD],
                                 lnT[:, kk * MACRO:(kk + 1) * MACRO],
                                 start=(kk == 0), stop=(kk == 7),
                                 reads=[tWt, tlnT], writes=[tb], inc=(kk == 7))
                        b.op("V", "tensor_copy", dst[:], bank[:, 0:MACRO],
                             reads=[tb], writes=[tdst])
                    for sub in range(NSUB):
                        for j in range(2):
                            bank, tb = PS[2 + 2 * sub + j]
                            b.op("T", "matmul", bank[:],
                                 ogh_bf[:, sub * 128:(sub + 1) * 128],
                                 Wog2_bf[:, j * 512:(j + 1) * 512],
                                 start=True, stop=True,
                                 reads=[togh, tWog2], writes=[tb])
                            b.op("S", "activation",
                                 gate_bf[:, sub * D + j * 512: sub * D + (j + 1) * 512],
                                 bank[:], AF.Sigmoid, reads=[tb], writes=[tgate])
                    bank, tb = PS[6]
                    for kk in range(8):
                        b.op("T", "matmul", bank[0:8, 0:MACRO],
                             Wbe_bf[:, kk * H:(kk + 1) * H],
                             lnT[:, kk * MACRO:(kk + 1) * MACRO],
                             start=(kk == 0), stop=(kk == 7),
                             reads=[tWbe, tlnT], writes=[tb], inc=(kk == 7))
                    b.op("S", "activation", bsig[:], bank[0:8, 0:MACRO], AF.Sigmoid,
                         reads=[tb], writes=[tbsig])
                    if not last_macro:
                        dma(Wbuf[0][0][:], wbf_dram["Wq"].ap(),
                            reads=[twbfd["Wq"]], writes=list(Wbuf[0][1]),
                            dsem="DP0")
                    for hp in range(4):
                        bank, tb = PS[hp % 2]
                        for j in range(2):
                            hh = hp * 2 + j
                            b.op("T", "matmul", bank[:, j * MACRO:(j + 1) * MACRO],
                                 Wf2_bf[:, hh * HD:(hh + 1) * HD], fh_bf[:],
                                 start=True, stop=True,
                                 reads=[tWf2, tfh], writes=[tb], inc=(j == 1))
                        b.op("S", "activation", scrA[:, 0:512], bank[:], AF.Exp,
                             scale=-1.0, reads=[tb], writes=[tscrA])
                        b.op("S", "activation", lam[:, hp * 512:(hp + 1) * 512],
                             scrA[:, 0:512], AF.Ln, bias=epsc[:, 3:4],
                             reads=[tscrA, tepsc], writes=[tlam])
                    for j in range(4):
                        b.op("S", "activation", KtP[j % 2][0][:],
                             k_f32[:, j * 512:(j + 1) * 512], AF.Square,
                             reads=[tk], writes=[KtP[j % 2][1]])
                        bank, tb = PS[4 + j % 2]
                        b.op("T", "matmul", bank[0:1, :], onescol[:],
                             KtP[j % 2][0][:], start=True, stop=True,
                             reads=[tones, KtP[j % 2][1]], writes=[tb])
                        srow = sqrow if j % 2 == 0 else sqrowB
                        tsrow = tsqrow if j % 2 == 0 else tsqrow2
                        sbfr = sqbf if j % 2 == 0 else sqbfB
                        tsbfr = tsqbf if j % 2 == 0 else tsqbf2
                        b.op("S", "activation", srow[0:1, :],
                             bank[0:1, :], AF.Ln, bias=epsc[0:1, 1:2],
                             reads=[tb, tepsc], writes=[tsrow])
                        b.op("S", "activation", sbfr[0:1, :],
                             srow[0:1, :], AF.Exp, scale=-0.5,
                             reads=[tsrow], writes=[tsbfr])
                        b.op("T", "matmul", bank[:], onescol[0:1, 0:1].broadcast_to((1, 128)),
                             sbfr[0:1, :], start=True, stop=True,
                             reads=[tones, tsbfr], writes=[tb])
                        b.op("V", "scalar_tensor_tensor",
                             khat[:, j * 512:(j + 1) * 512],
                             bank[:], 1.0,
                             k_f32[:, j * 512:(j + 1) * 512],
                             ALU.mult, ALU.mult,
                             reads=[tb, tk], writes=[tkh])
                    lam4 = lam[:].rearrange("p (h c t) -> p h c t", h=H, c=NCH)
                    Lam4 = Lam[:].rearrange("p (h c t) -> p h c t", h=H, c=NCH)
                    for cc in range(NCH):
                        for hh in range(H):
                            b.op("V", "tensor_tensor_scan",
                                 Lam4[:, hh, cc, :], lam4[:, hh, cc, :],
                                 lam4[:, hh, cc, :], 0.0, ALU.add, ALU.bypass,
                                 reads=[tlam], writes=[tLamC[cc]],
                                 inc=(hh == H - 1))
                    for hh in range(H):
                        bank, tb = PS[6 + hh % 2]
                        b.op("T", "matmul", bank[:, 0:MACRO],
                             bcast8[:, hh * 128:(hh + 1) * 128], bsig[:],
                             start=True, stop=True,
                             reads=[tbc8, tbsig], writes=[tb])
                        b.op("S", "copy", sigbc[:, hh * MACRO:(hh + 1) * MACRO],
                             bank[:, 0:MACRO], reads=[tb], writes=[tsig])
                    _beng = ["P"]
                    def chunk_build_elem(cc):
                        sub, half = cc // 2, cc % 2
                        Bm, tBm = BmP[cc % 2]
                        Am, tAm = AmP[cc % 2]
                        Kt, tKt = KtP[cc % 2]
                        qt, tqt = qtP[cc % 2]
                        Atm, tAtm = AtmP[cc % 2]
                        Ktm, tKtm = KtmP[cc % 2]
                        egc, tegc = egcP[cc % 2]
                        lexc, tlexc = lexcP[cc % 2]
                        b.op(_beng[0], "tensor_tensor", bl3(lexc), chunk3(Lam, cc),
                             chunk3(lam, cc), ALU.subtract,
                             reads=[tLamC[cc], tlam], writes=[tlexc])
                        BmR = Bm[:].rearrange("p (h t) -> p h t", h=H)
                        AmR = Am[:].rearrange("p (h t) -> p h t", h=H)
                        b.op("S", "activation", bl3(eX), bl3(lexc), AF.Exp,
                             scale=-1.0, reads=[tlexc], writes=[teX])
                        b.op(_beng[0], "tensor_tensor", BmR[:, :, 0:C], chunk3(khat, cc),
                             bl3(eX), ALU.mult, reads=[tkh, teX], writes=[tBm])
                        b.op("S", "activation", bl3(scrA[:, 0:512]), chunk3(Lam, cc), AF.Exp,
                             scale=-1.0, reads=[tLamC[cc]], writes=[teXB, tscrA])
                        b.op(_beng[0], "tensor_tensor", BmR[:, :, C:L], chunk3(khat, cc),
                             bl3(scrA[:, 0:512]), ALU.mult, reads=[tkh, teXB], writes=[tBm])
                        b.op(_beng[0], "tensor_tensor", bl3(qt), chunk3(q_bf, cc),
                             bl3(scrA[:, 0:512]), ALU.mult, reads=[tq, teXB], writes=[tqt])
                        b.op("S", "activation", bl3(eX), bl3(lexc), AF.Exp,
                             reads=[tlexc], writes=[teX])
                        b.op(_beng[0], "tensor_tensor", bl3(t1s), chunk3(khat, cc),
                             bl3(eX), ALU.mult, reads=[tkh, teX], writes=[tt1])
                        b.op("S", "activation", bl3(scrA[:, 0:512]), chunk3(Lam, cc), AF.Exp,
                             reads=[tLamC[cc]], writes=[teXB, tscrA])
                        b.op(_beng[0], "tensor_tensor", bl3(Kt), chunk3(khat, cc),
                             bl3(scrA[:, 0:512]), ALU.mult, reads=[tkh, teXB], writes=[tKt])
                        b.op(_beng[0], "tensor_tensor", AmR[:, :, 0:C],
                             chunk3(sigbc, cc), bl3(t1s), ALU.mult,
                             reads=[tsig, tt1], writes=[tAm])
                        b.op(_beng[0], "tensor_tensor", AmR[:, :, C:L],
                             chunk3(sigbc, cc), bl3(Kt), ALU.mult,
                             reads=[tsig, tKt], writes=[tAm])
                        b.op(_beng[0], "tensor_copy", egc[:, 0:8],
                             Lam4[:, :, cc, C - 1], reads=[tLamC[cc]], writes=[tegc])
                        b.op("S", "activation", egc[:, 0:8], egc[:, 0:8], AF.Exp,
                             scale=-1.0, reads=[tegc], writes=[tegc])
                        if last_macro and cc == 0:
                            tap("Bm", Bm[:], [tBm]); tap("Am", Am[:], [tAm])

                    def chunk_build_tr(cc):
                        sub, half = cc // 2, cc % 2
                        Bm, tBm = BmP[cc % 2]
                        Am, tAm = AmP[cc % 2]
                        Kt, tKt = KtP[cc % 2]
                        qt, tqt = qtP[cc % 2]
                        Atm, tAtm = AtmP[cc % 2]
                        Ktm, tKtm = KtmP[cc % 2]
                        egc, tegc = egcP[cc % 2]
                        lexc, tlexc = lexcP[cc % 2]
                        pb, tpb = PS[6]
                        pe_transpose_8(
                            lambda i: Am[:, i * 128:(i + 1) * 128],
                            lambda i: Atm[:, i * 128:(i + 1) * 128],
                            tAm, tAtm, pb, tpb)
                        pb2, tpb2 = PS[7]
                        pe_transpose_8(
                            lambda i: Kt[:, i * 64:(i + 1) * 64],
                            lambda i: Ktm[:, i * 128:(i + 1) * 128],
                            tKt, tKtm, pb2, tpb2, rows=64)

                    def chunk_build_ops(cc):
                        opsE, opsT = [], []
                        real_op = b.op
                        b.op = lambda *a, **k: opsE.append((a, k))
                        try:
                            chunk_build_elem(cc)
                        finally:
                            b.op = real_op
                        b.op = lambda *a, **k: opsT.append((a, k))
                        try:
                            chunk_build_tr(cc)
                        finally:
                            b.op = real_op
                        return opsE, opsT

                    _beng[0] = "V"
                    chunk_build_elem(0)
                    _beng[0] = "P"
                    chunk_build_tr(0)
                    if last_macro:
                        tap("q", q_bf[:], [tq]); tap("k", k_f32[:], [tk])
                        tap("khat", khat[:], [tkh]); tap("lam", lam[:], [tlam])
                        tap("Lam", Lam[:], tLamC); tap("sig", sigbc[:], [tsig])
                        tap("v", v_bf[:], [tv]); tap("gate", gate_bf[:], [tgate])
                    for cc in range(NCH):
                        if cc == 2:
                            emit_preA(mi + 1)
                        sub, half = cc // 2, cc % 2
                        Bm, tBm = BmP[cc % 2]
                        Am, tAm = AmP[cc % 2]
                        Kt, tKt = KtP[cc % 2]
                        qt, tqt = qtP[cc % 2]
                        Atm, tAtm = AtmP[cc % 2]
                        Ktm, tKtm = KtmP[cc % 2]
                        egc, tegc = egcP[cc % 2]
                        lexc, tlexc = lexcP[cc % 2]
                        GRP = [list(range(0, 4)), list(range(4, 8))]
                        XBs = [PS[2], PS[4]]
                        PBs = [PS[3], PS[5]]
                        WBs = [PS[6], PS[7]]
                        UBs = [PS[0], PS[1]]
                        for g in range(2):
                            XB, tXB = XBs[g]
                            PB, tPB = PBs[g]
                            for i, hh in enumerate(GRP[g]):
                                b.op("T", "matmul",
                                     XB[:, i * 128:(i + 1) * 128],
                                     Bm[:, hh * 128:(hh + 1) * 128],
                                     Am[:, hh * 128:(hh + 1) * 128],
                                     start=True, stop=True,
                                     reads=[tBm, tAm], writes=[tXB], inc=(i == 3))
                            for i, hh in enumerate(GRP[g]):
                                b.op("T", "matmul",
                                     PB[:, i * 128:(i + 1) * 128],
                                     Am[:, hh * 128:(hh + 1) * 128],
                                     Bm[:, hh * 128:(hh + 1) * 128],
                                     start=True, stop=True,
                                     reads=[tBm, tAm], writes=[tPB], inc=(i == 3))
                        for g in range(2):
                            b.op("V", "tensor_tensor", T_sb[g][0][:],
                                 XBs[g][0][:],
                                 consts[:, 0:512], ALU.mult,
                                 reads=[XBs[g][1], tconsts], writes=[T_sb[g][1]])
                            b.op("V", "tensor_tensor", Pm_sb[g][0][:],
                                 PBs[g][0][:],
                                 consts[:, 512:1024], ALU.mult,
                                 reads=[PBs[g][1], tconsts], writes=[Pm_sb[g][1]])
                        if last_macro and cc == 0:
                            tap("Tm", T_sb[0][0][:], [T_sb[0][1]])
                        for g in range(2):
                            WB, tWB = WBs[g]
                            for i, hh in enumerate(GRP[g]):
                                b.op("T", "matmul",
                                     WB[0:64, i * 128:(i + 1) * 128],
                                     Kt[:, hh * 64:(hh + 1) * 64],
                                     Bm[:, hh * 128:(hh + 1) * 128],
                                     start=True, stop=True,
                                     reads=[tKt, tBm], writes=[tWB], inc=(i == 3))
                        for g in range(2):
                            b.op("V", "tensor_tensor", W_sb[:, g * 512:(g + 1) * 512],
                                 WBs[g][0][0:64, :], consts2[:, 0:512], ALU.mult,
                                 reads=[WBs[g][1], tconsts2], writes=[tWsb])
                        for g in range(2):
                            HB, tHB = WBs[g]
                            for i, hh in enumerate(GRP[g]):
                                b.op("T", "matmul",
                                     HB[:, i * 64:(i + 1) * 64],
                                     Am[:, hh * 128:(hh + 1) * 128],
                                     qt[:, hh * 64:(hh + 1) * 64],
                                     start=True, stop=True,
                                     reads=[tAm, tqt], writes=[tHB], inc=(i == 3))
                            for i, hh in enumerate(GRP[g]):
                                b.op("T", "matmul",
                                     HB[0:64, 256 + i * 64: 256 + (i + 1) * 64],
                                     Kt[:, hh * 64:(hh + 1) * 64],
                                     qt[:, hh * 64:(hh + 1) * 64],
                                     start=True, stop=True,
                                     reads=[tKt, tqt], writes=[tHB], inc=(i == 3))
                        for g in range(2):
                            b.op("V", "tensor_tensor",
                                 HQ_sb[:, g * 256:(g + 1) * 256],
                                 WBs[g][0][:, 0:256],
                                 consts[:, 1024:1280], ALU.mult,
                                 reads=[WBs[g][1], tconsts], writes=[tHQ])
                            b.op("V", "tensor_tensor",
                                 KQ_sb[:, g * 256:(g + 1) * 256],
                                 WBs[g][0][0:64, 256:512],
                                 consts2[:, 512:768], ALU.mult,
                                 reads=[WBs[g][1], tconsts2], writes=[tKQ])
                        for g in range(2):
                            UB, tUB = UBs[g]
                            for i, hh in enumerate(GRP[g]):
                                sl = s * 8 + hh
                                b.op("T", "matmul",
                                     UB[:, i * 128:(i + 1) * 128],
                                     Bm[:, hh * 128:(hh + 1) * 128],
                                     S0_bf[:, sl * 128:(sl + 1) * 128],
                                     start=True, stop=False,
                                     reads=[tBm, tS0b], writes=[tUB], inc=False)
                                b.op("T", "matmul",
                                     UB[:, i * 128:(i + 1) * 128],
                                     W_sb[:, g * 512 + i * 128: g * 512 + (i + 1) * 128],
                                     v_bf[:, cc * D + hh * 128: cc * D + (hh + 1) * 128],
                                     start=False, stop=True,
                                     reads=[tWsb, tv], writes=[tUB], inc=(i == 3))
                        opsE, opsT = (chunk_build_ops(cc + 1)
                                      if cc + 1 < NCH else ([], []))
                        for a_, k_ in opsE:
                            b.op(*a_, **k_)
                        for lev in range(7):
                            for g in range(2):
                                b.op("S", "copy", U_sb[:, g * 512:(g + 1) * 512],
                                     UBs[g][0][:],
                                     reads=[UBs[g][1]], writes=[tUg[g]])
                            if lev < 5:
                                for g in range(2):
                                    XB, tXB = XBs[g]
                                    srcX, tsrcX = ((T_sb[g][0], T_sb[g][1]) if lev == 0
                                                   else (X_sb[g][0], X_sb[g][1]))
                                    for i in range(4):
                                        b.op("T", "matmul",
                                             XB[:, i * 128:(i + 1) * 128],
                                             Pm_sb[g][0][:, i * 128:(i + 1) * 128],
                                             srcX[:, i * 128:(i + 1) * 128],
                                             start=True, stop=True,
                                             reads=[Pm_sb[g][1], tsrcX], writes=[tXB],
                                             inc=(i == 3))
                            for g in range(2):
                                UB, tUB = UBs[g]
                                for i, hh in enumerate(GRP[g]):
                                    b.op("T", "matmul",
                                         UB[:, i * 128:(i + 1) * 128],
                                         IDB, U_sb[:, hh * 128:(hh + 1) * 128],
                                         start=True, stop=False,
                                         reads=[tconsts, tUg[g]], writes=[tUB], inc=False)
                                    b.op("T", "matmul",
                                         UB[:, i * 128:(i + 1) * 128],
                                         Pm_sb[g][0][:, i * 128:(i + 1) * 128],
                                         U_sb[:, hh * 128:(hh + 1) * 128],
                                         start=False, stop=True,
                                         reads=[Pm_sb[g][1], tUg[g]], writes=[tUB],
                                         inc=(i == 3))
                            if lev < 6:
                                for g in range(2):
                                    PB, tPB = PBs[g]
                                    srcX, tsrcX = ((T_sb[g][0], T_sb[g][1]) if lev == 0
                                                   else (X_sb[g][0], X_sb[g][1]))
                                    for i in range(4):
                                        b.op("T", "matmul",
                                             PB[:, i * 128:(i + 1) * 128],
                                             srcX[:, i * 128:(i + 1) * 128],
                                             Pm_sb[g][0][:, i * 128:(i + 1) * 128],
                                             start=True, stop=True,
                                             reads=[Pm_sb[g][1], tsrcX], writes=[tPB],
                                             inc=(i == 3))
                                for g in range(2):
                                    ce = "V" if g == 0 else "S"
                                    cf = "tensor_copy" if g == 0 else "copy"
                                    if lev < 5:
                                        b.op(ce, cf, X_sb[g][0][:],
                                             XBs[g][0][:],
                                             reads=[XBs[g][1]], writes=[X_sb[g][1]])
                                    b.op(ce, cf, Pm_sb[g][0][:],
                                         PBs[g][0][:],
                                         reads=[PBs[g][1]], writes=[Pm_sb[g][1]])
                        for g in range(2):
                            b.op("S", "copy", U_sb[:, g * 512:(g + 1) * 512],
                                 UBs[g][0][:],
                                 reads=[UBs[g][1]], writes=[tUg[g]])
                        if last_macro and cc == 0:
                            tap("U", U_sb[:], [tUg[0], tUg[1]])
                        for g in range(2):
                            OB, tOB = XBs[g]
                            for i, hh in enumerate(GRP[g]):
                                sl = s * 8 + hh
                                b.op("T", "matmul", OB[0:64, i * 128:(i + 1) * 128],
                                     qt[:, hh * 64:(hh + 1) * 64],
                                     S0_bf[:, sl * 128:(sl + 1) * 128],
                                     start=True, stop=False,
                                     reads=[tqt, tS0b], writes=[tOB], inc=False)
                                b.op("T", "matmul", OB[0:64, i * 128:(i + 1) * 128],
                                     HQ_sb[:, g * 256 + i * 64: g * 256 + (i + 1) * 64],
                                     U_sb[:, hh * 128:(hh + 1) * 128],
                                     start=False, stop=False,
                                     reads=[tHQ, tUg[g]], writes=[tOB], inc=False)
                                b.op("T", "matmul", OB[0:64, i * 128:(i + 1) * 128],
                                     KQ_sb[:, g * 256 + i * 64: g * 256 + (i + 1) * 64],
                                     v_bf[:, cc * D + hh * 128: cc * D + (hh + 1) * 128],
                                     start=False, stop=True,
                                     reads=[tKQ, tv], writes=[tOB], inc=(i == 3))
                        for g in range(2):
                            b.op("S", "copy",
                                 y_sb[half * 64:(half + 1) * 64,
                                      sub * D + g * 512: sub * D + (g + 1) * 512],
                                 XBs[g][0][0:64, :], reads=[XBs[g][1]], writes=[ty])
                        for a_, k_ in opsT:
                            b.op(*a_, **k_)
                        for g in range(2):
                            SB, tSB = PBs[g]
                            for i, hh in enumerate(GRP[g]):
                                b.op("T", "matmul", SB[:, i * 128:(i + 1) * 128],
                                     Atm[:, hh * 128:(hh + 1) * 128],
                                     U_sb[:, hh * 128:(hh + 1) * 128],
                                     start=True, stop=False,
                                     reads=[tAtm, tUg[g]], writes=[tSB], inc=False)
                                b.op("T", "matmul", SB[:, i * 128:(i + 1) * 128],
                                     Ktm[:, hh * 128:(hh + 1) * 128],
                                     v_bf[:, cc * D + hh * 128: cc * D + (hh + 1) * 128],
                                     start=False, stop=True,
                                     reads=[tKtm, tv], writes=[tSB], inc=(i == 3))
                        for g in range(2):
                            sl0 = s * 8 + 4 * g
                            b.op("V", "tensor_tensor", Ssc, PBs[g][0][:],
                                 S0_f32[:, sl0 * 128:(sl0 + 4) * 128], ALU.add,
                                 reads=[PBs[g][1], tS0], writes=[tSsc])
                            b.op("V", "tensor_tensor",
                                 S0_f32[:, sl0 * 128:(sl0 + 4) * 128].rearrange(
                                     "p (l d) -> p l d", l=4),
                                 Ssc.rearrange("p (l d) -> p l d", l=4),
                                 egc[:, 4 * g:4 * g + 4].rearrange(
                                     "p (l o) -> p l o", o=1).broadcast_to(
                                     (128, 4, 128)),
                                 ALU.mult, reads=[tSsc, tegc], writes=[tS0])
                            b.op("S", "copy", S0_bf[:, sl0 * 128:(sl0 + 4) * 128],
                                 S0_f32[:, sl0 * 128:(sl0 + 4) * 128],
                                 reads=[tS0], writes=[tS0b])

                    for j in range(2):
                        b.op("V", "tensor_tensor", scrA[:, j * 1024:(j + 1) * 1024],
                             y_sb[:, j * 1024:(j + 1) * 1024],
                             gate_bf[:, j * 1024:(j + 1) * 1024], ALU.mult,
                             reads=[ty, tgate], writes=[tscrA])
                    if last_macro:
                        tap("y", scrA[:, 0:2048], [tscrA])
                    for j in range(2):
                        b.op("S", "activation", y_sb[:, j * 1024:(j + 1) * 1024],
                             scrA[:, j * 1024:(j + 1) * 1024], AF.Square,
                             reads=[tscrA], writes=[ty])
                    for sub in range(NSUB):
                        b.op("V", "reduce_sum",
                             rmscol[:, sub * H:(sub + 1) * H].rearrange(
                                 "p (h o) -> p h o", o=1),
                             y_sb[:, sub * D:(sub + 1) * D].rearrange(
                                 "p (h d) -> p h d", h=H),
                             axis=AX, reads=[ty], writes=[trms])
                    b.op("S", "activation", rmscol[:], rmscol[:], AF.Ln,
                         scale=1.0 / HD, bias=epsc[:, 2:3], reads=[trms, tepsc],
                         writes=[trms])
                    b.op("S", "activation", rmscol[:], rmscol[:], AF.Exp,
                         scale=-0.5, reads=[trms], writes=[trms])
                    for sub in range(NSUB):
                        b.op("V", "tensor_tensor",
                             yT_bf[:, sub * D:(sub + 1) * D].rearrange(
                                 "p (h d) -> p h d", h=H),
                             scrA[:, sub * D:(sub + 1) * D].rearrange(
                                 "p (h d) -> p h d", h=H),
                             rmscol[:, sub * H:(sub + 1) * H].rearrange(
                                 "p (h o) -> p h o", o=1).broadcast_to((128, H, HD)),
                             ALU.mult, reads=[tscrA, trms], writes=[tyT])
                    emit_qproj(mi + 1)
                    for grp in range(2):
                        pbank, tpbank = PS[7]
                        idx = [(grp * 4 + i % 4, i // 4) for i in range(8)]
                        for i in range(8):
                            kk, sub = idx[i]
                            b.op("T", "transpose",
                                 pbank[:].bitcast(BF16)[:, i * 128:(i + 1) * 128],
                                 yT_bf[:, sub * D + kk * 128:
                                       sub * D + (kk + 1) * 128], IDB,
                                 reads=[tyT, tconsts], writes=[tpbank],
                                 inc=(i == 7))
                        nwb = bass.AP(nw_col.handle if hasattr(nw_col, "handle")
                                      else nw_col, 0, [[8, 128], [1, 8]]) \
                            if False else nw_col[:]
                        nw4 = nwb[:, grp * 4:(grp + 1) * 4]
                        nw_b = nw4.rearrange("p (k o) -> p k o", o=1)
                        nw_b = nw_b.unsqueeze(2).broadcast_to((128, 4, 2, 128)) \
                            if hasattr(nw_b, "unsqueeze") else nw_b
                        b.op("V", "tensor_tensor", _perm_out(BmP[grp][0], 0),
                             _perm_in(pbank), nw_b, ALU.mult,
                             reads=[tpbank, tnw], writes=[BmP[grp][1]])
                    for sub in range(NSUB):
                        for j in range(2):
                            bank, tb = PS[4 + sub * 2 + j]
                            for kk in range(8):
                                b.op("T", "matmul", bank[:],
                                     BmP[kk // 4][0][:, (kk % 4) * 256 + sub * 128:
                                                     (kk % 4) * 256 + (sub + 1) * 128],
                                     Wbuf[1][0][:, kk * D + j * 512:
                                                 kk * D + (j + 1) * 512],
                                     start=(kk == 0), stop=(kk == 7),
                                     reads=[BmP[kk // 4][1], Wbuf[1][1][kk]],
                                     writes=[tb], inc=(kk == 7))
                            b.op("V", "tensor_tensor",
                                 x_sb[:, sub * D + j * 512: sub * D + (j + 1) * 512],
                                 bank[:],
                                 x_sb[:, sub * D + j * 512: sub * D + (j + 1) * 512],
                                 ALU.add, reads=[tb, tx], writes=[tx])
                        dma(x2_dram.ap()[s, t0 + sub * 128: t0 + (sub + 1) * 128, :],
                            x_sb[:, sub * D:(sub + 1) * D], reads=[tx],
                            writes=[tx2d], dsem="DSP")
                    if not last_macro:
                        for kk in range(8):
                            dma(Wbuf[1][0][:, kk * D:(kk + 1) * D],
                                wbf_dram["Wk"].ap()[:, kk * D:(kk + 1) * D],
                                reads=[twbfd["Wk"]], writes=[Wbuf[1][1][kk]],
                                dsem=f"DK{kk}")
                    if last_macro:
                        tap("x2", x_sb[:], [tx])
                        tap("S0", S0_f32[:], [tS0])
                b.flush(blk)
            blk.__exit__(None, None, None)

        with ExitStack() as ctxB:
            W1_bf = ctxB.enter_context(nc.sbuf_tensor("W1_bf", [128, 8 * FFN], BF16))
            tW1 = [[Buf(f"W1.{k}.{p}") for p in range(4)] for k in range(8)]
            W2_bf = ctxB.enter_context(nc.sbuf_tensor("W2_bf", [128, 32 * D], BF16))
            tW2 = [Buf(f"W2.{k}") for k in range(32)]
            xPB = [xP[0], mk(ctxB, "x_sbB", [128, NSUB * D], F32)]
            lnTPB = [lnTP[0], mk(ctxB, "lnTB", [128, 8 * MACRO], BF16)]
            yTPB = [yTP[0], mk(ctxB, "yT_bfB", [128, NSUB * D], BF16)]
            b1c, tb1c = mk(ctxB, "b1c", [128, 32], F32)
            b2bc, tb2bc = mk(ctxB, "b2bc", [128, D], F32)
            hfm, thfm = mk(ctxB, "hfm", [128, 32 * MACRO], BF16)

            with nc.Block() as blk:
                for e in ENGS:
                    b.wait_everything(e)
                dma(wstage[0:32, 0:128],
                    wh["b1"].ap().rearrange("(a b) -> a b", a=32),
                    writes=[twstage], dsem="DW0")
                pb, tpb = PS[0]
                b.op("T", "transpose", pb[:, 0:32], wstage[0:32, 0:128],
                     ident32[0:32, 0:32], reads=[twstage, tid32], writes=[tpb])
                b.op("V", "tensor_copy", b1c[:], pb[:, 0:32], reads=[tpb],
                     writes=[tb1c])
                dma(wstage[0:1, 0:1024], wh["b2"].ap()[None, :],
                    writes=[twstage], dsem="DW0")
                for j in range(2):
                    bank, tb = PS[1 + j]
                    b.op("T", "matmul", bank[:], onesrow[:],
                         wstage[0:1, j * 512:(j + 1) * 512], start=True, stop=True,
                         reads=[tonesr, twstage], writes=[tb])
                    b.op("V", "tensor_copy", b2bc[:, j * 512:(j + 1) * 512],
                         bank[:], reads=[tb], writes=[tb2bc])
                schain = [Buf(f"schain{i}") for i in range(8)]
                _sc = [0]
                def sdma(dst, srcap, rd, wr):
                    c = _sc[0] % 8; _sc[0] += 1
                    dma(dst, srcap, reads=rd, writes=wr + [schain[c]],
                        dsem=f"DS{c}")
                for piece in range(4):
                    for kk in range(8):
                        sdma(W1_bf[:, kk * FFN + piece * 1024:
                                   kk * FFN + (piece + 1) * 1024],
                             wbf_dram["w1"].ap()[kk * 128:(kk + 1) * 128,
                                                 piece * 1024:(piece + 1) * 1024],
                             [twbfd["w1"]], [tW1[kk][piece]])
                for kk in range(32):
                    sdma(W2_bf[:, kk * D:(kk + 1) * D],
                         wbf_dram["w2"].ap()[kk * 128:(kk + 1) * 128, :],
                         [twbfd["w2"]], [tW2[kk]])
                def emit_preB_ln(mj):
                    if mj >= n_macros:
                        return
                    s2 = mj % SEQ_PER_CORE
                    t02 = (mj // SEQ_PER_CORE) * MACRO
                    x2b, tx2b = xPB[mj % 2]
                    yT2, tyT2 = yTPB[mj % 2]
                    for sub in range(NSUB):
                        dma(x2b[:, sub * D:(sub + 1) * D],
                            x2_dram.ap()[s2, t02 + sub * 128:
                                         t02 + (sub + 1) * 128, :],
                            reads=[tx2d], writes=[tx2b],
                            dsem=("DXB0" if mj % 2 == 0 else "DXB1"))
                    for sub in range(NSUB):
                        layernorm(x2b[:, sub * D:(sub + 1) * D],
                                  yT2[:, sub * D:(sub + 1) * D], tyT2, tx2b,
                                  yT2[:, sub * D:(sub + 1) * D], tyT2)
                def emit_preB_tr(mj):
                    if mj >= n_macros:
                        return
                    lnT2, tlnT2 = lnTPB[mj % 2]
                    yT2, tyT2 = yTPB[mj % 2]
                    for grp in range(2):
                        pbank, tpbank = PS[0]
                        idx = [(grp * 4 + i % 4, i // 4) for i in range(8)]
                        pe_transpose_8(
                            lambda i: yT2[:, idx[i][1] * D + idx[i][0] * 128:
                                          idx[i][1] * D + (idx[i][0] + 1) * 128],
                            None, tyT2, tlnT2, pbank, tpbank, evac_eng="S",
                            dst_batched=(_perm_out(lnT2, grp), _perm_in(pbank)))
                emit_preB_ln(0)
                emit_preB_tr(0)
                for mi in range(n_macros):
                    s = mi % SEQ_PER_CORE
                    mb = mi // SEQ_PER_CORE
                    t0 = mb * MACRO
                    x_sb, tx = xPB[mi % 2]
                    lnT, tlnT = lnTPB[mi % 2]
                    yT_bf, tyT = yTPB[mi % 2]
                    emit_preB_ln(mi + 1)
                    for hs in range(32):
                        bank, tb = PS[hs % 4]
                        for kk in range(8):
                            b.op("T", "matmul", bank[:, 0:MACRO],
                                 W1_bf[:, kk * FFN + hs * 128:
                                       kk * FFN + (hs + 1) * 128],
                                 lnT[:, kk * MACRO:(kk + 1) * MACRO],
                                 start=(kk == 0), stop=(kk == 7),
                                 reads=[tW1[kk][hs // 8], tlnT], writes=[tb], inc=(kk == 7))
                        b.op("S", "activation", hfm[:, hs * MACRO:(hs + 1) * MACRO],
                             bank[:, 0:MACRO], AF.Relu, bias=b1c[:, hs:hs + 1],
                             reads=[tb, tb1c], writes=[thfm])
                    for sub in range(NSUB):
                        for j in range(2):
                            bank, tb = PS[4 + sub * 2 + j]
                            for hs in range(32):
                                b.op("T", "matmul", bank[:],
                                     hfm[:, hs * MACRO + sub * 128:
                                         hs * MACRO + (sub + 1) * 128],
                                     W2_bf[:, hs * D + j * 512: hs * D + (j + 1) * 512],
                                     start=(hs == 0), stop=(hs == 31),
                                     reads=[thfm, tW2[hs]], writes=[tb], inc=(hs == 31))
                            b.op("V", "tensor_tensor",
                                 y_sb[:, sub * D + j * 512: sub * D + (j + 1) * 512],
                                 bank[:], b2bc[:, j * 512:(j + 1) * 512], ALU.add,
                                 reads=[tb, tb2bc], writes=[ty])
                            b.op("P", "tensor_tensor",
                                 y_sb[:, sub * D + j * 512: sub * D + (j + 1) * 512],
                                 y_sb[:, sub * D + j * 512: sub * D + (j + 1) * 512],
                                 x_sb[:, sub * D + j * 512: sub * D + (j + 1) * 512],
                                 ALU.add, reads=[ty, tx], writes=[ty])
                        dma(out_h.ap()[s, t0 + sub * 128: t0 + (sub + 1) * 128, :],
                            y_sb[:, sub * D:(sub + 1) * D], reads=[ty],
                            writes=[toutc], dsem="DO")
                    emit_preB_tr(mi + 1)
                b.wait_everything("Y")
                b.flush(blk)

    return nc


def _in_maps(x, weights):
    maps = []
    for c in range(NCORES):
        m = {"x": np.ascontiguousarray(x[c * SEQ_PER_CORE:(c + 1) * SEQ_PER_CORE])}
        m.update(weights)
        maps.append(m)
    return maps


def kernel(L0_attn_x, L0_attn_Wq, L0_attn_Wk, L0_attn_Wv, L0_attn_Wf1,
           L0_attn_Wf2, L0_attn_Wbeta, L0_attn_Wog1, L0_attn_Wog2,
           L0_attn_norm_w, L0_attn_Wo, L0_ffn_w1, L0_ffn_b1, L0_ffn_w2,
           L0_ffn_b2):
    x = np.ascontiguousarray(np.asarray(L0_attn_x, dtype=np.float32))
    weights = {
        "Wq": L0_attn_Wq, "Wk": L0_attn_Wk, "Wv": L0_attn_Wv,
        "Wf1": L0_attn_Wf1, "Wf2": L0_attn_Wf2, "Wbeta": L0_attn_Wbeta,
        "Wog1": L0_attn_Wog1, "Wog2": L0_attn_Wog2, "norm_w": L0_attn_norm_w,
        "Wo": L0_attn_Wo, "w1": L0_ffn_w1, "b1": L0_ffn_b1,
        "w2": L0_ffn_w2, "b2": L0_ffn_b2,
    }
    weights = {k: np.ascontiguousarray(np.asarray(v, dtype=np.float32))
               for k, v in weights.items()}
    nc = build_program()
    res = run_bass_kernel_spmd(nc, _in_maps(x, weights),
                               core_ids=list(range(NCORES)))
    return np.concatenate([r["out"] for r in res.results], axis=0)
